# version 02
# baseline (speedup 1.0000x reference)
import contextlib
import numpy as np
import ml_dtypes

import concourse.bass as bass
import concourse.mybir as mybir
from concourse.bass import IndirectOffsetOnAxis
from concourse.bass_utils import run_bass_kernel_spmd

BF = ml_dtypes.bfloat16
F32 = mybir.dt.float32
BF16 = mybir.dt.bfloat16
I32 = mybir.dt.int32
U32 = mybir.dt.uint32
AF = mybir.ActivationFunctionType
ALU = mybir.AluOpType
AX = mybir.AxisListType

T = 2048
D = 1024
NT = 16
H = 16
E = 16
CAP = 1024
F_FF = 4096
SHIFT_OFFS = [8, 4, 2, 1, 0]

ENGS = ("sync", "tensor", "vector", "scalar", "gpsimd")


class Prog:

    def __init__(self, nc):
        self.nc = nc
        self.items = {e: [] for e in ENGS}
        self.sems = {}
        self.pending = {e: [] for e in ENGS}
        self._bmark = {e: 0 for e in ENGS}

    def _sem(self, name):
        if name not in self.sems:
            self.sems[name] = [self.nc.alloc_semaphore(name), 0]
        return name

    CHAINED = ("vector", "scalar")

    def op(self, eng, fn, waits=(), incs=(), dma=False, cc=False):
        toks = []
        rec = []
        chain = eng in self.CHAINED and not dma and not cc
        if chain:
            s = self._sem(eng + ":chain")
            prev = self.sems[s][1]
            self.sems[s][1] += 1
            rec.append((s, 1))
            toks.append((s, self.sems[s][1]))
            waits = tuple(waits) + ((s, prev),)
            for _ in incs:
                toks.append((s, self.sems[s][1]))
        else:
            for s0 in incs:
                s = self._sem(eng + ":" + s0)
                if cc:
                    self.sems[s][1] += 1
                    rec.append((s, None))
                else:
                    d = 16 if dma else 1
                    self.sems[s][1] += d
                    rec.append((s, d))
                toks.append((s, self.sems[s][1]))
        w = tuple(self.pending[eng]) + tuple(t for t in waits if t is not None)
        self.pending[eng] = []
        self.items[eng].append([fn, w, tuple(rec), dma or cc])
        if not toks:
            return None
        return toks[0] if len(toks) == 1 else toks[0] if chain else toks

    def token(self, eng, name="tk"):
        if eng in self.CHAINED:
            s = eng + ":chain"
            return (s, self.sems[s][1])
        s = self._sem(eng + ":" + name)
        item = self.items[eng][-1]
        d = 16 if item[3] else 1
        self.sems[s][1] += d
        item[2] = item[2] + ((s, d),)
        return (s, self.sems[s][1])

    def barrier(self, name):
        for e in ENGS:
            if len(self.items[e]) > self._bmark[e]:
                item = self.items[e][-1]
                if not item[2]:
                    s = self._sem("B:" + name + e)
                    d = 16 if item[3] else 1
                    item[2] = item[2] + ((s, d),)
                    self.sems[s][1] += d
        snap = getattr(self, "_bsnap", {})
        toks = [(s, v[1]) for s, v in self.sems.items() if v[1] > snap.get(s, 0)]
        self._bsnap = {s: v[1] for s, v in self.sems.items()}
        for e in ENGS:
            self.pending[e].extend(toks)
            self._bmark[e] = len(self.items[e])
        return toks

    def emit(self):
        with self.nc.Block() as block:
            for eng_name in ENGS:
                items = self.items[eng_name]

                def make_body(items=items):
                    def body(eng):
                        done = {}
                        for fn, waits, incs, _ in items:
                            wmax = {}
                            for (s, v) in waits:
                                if v > wmax.get(s, -1):
                                    wmax[s] = v
                            for s, v in wmax.items():
                                if done.get(s, -1) < v:
                                    eng.wait_ge(self.sems[s][0], v)
                                    done[s] = v
                            inst = fn(eng)
                            for (s, d) in incs:
                                if d is None:
                                    inst.then_inc(self.sems[s][0])
                                else:
                                    inst.then_inc(self.sems[s][0], d)
                    return body

                getattr(block, eng_name)(make_body())


def dma(out, in_, **kw):
    def fn(eng):
        return eng.dma_start(out=out, in_=in_, **kw)
    return fn


def dmat(out, in_):
    def fn(eng):
        return eng.dma_start_transpose(out, in_)
    return fn


def vop(fn_name, *a, **kw):
    def fn(eng):
        return getattr(eng, fn_name)(*a, **kw)
    return fn


def igather(out, in_, idx):
    def fn(eng):
        return eng.indirect_dma_start(
            out=out, out_offset=None, in_=in_,
            in_offset=IndirectOffsetOnAxis(ap=idx, axis=0))
    return fn


def iscatter(out, idx, in_):
    def fn(eng):
        return eng.indirect_dma_start(
            out=out, out_offset=IndirectOffsetOnAxis(ap=idx, axis=0),
            in_=in_, in_offset=None)
    return fn


def collective(kind, op_, ins, outs):
    def fn(eng):
        return eng.collective_compute(
            kind, op_, replica_groups=[list(range(8))], ins=ins, outs=outs)
    return fn


STAGE = [99]
MSTAGE = [99]


def build(n_sub=4):
    nc = bass.Bass("TRN2", target_bir_lowering=False)
    p = Prog(nc)

    x_in = nc.dram_tensor("x", [T, D], F32, kind="ExternalInput").ap()
    x_out = nc.dram_tensor("xout", [T, D], F32, kind="ExternalOutput").ap()
    wqkv = nc.dram_tensor("wqkv", [D, 3 * D], F32, kind="ExternalInput").ap()
    bqkv = nc.dram_tensor("bqkv", [1, 3 * D], F32, kind="ExternalInput").ap()
    wproj = nc.dram_tensor("wproj", [D, D], F32, kind="ExternalInput").ap()
    bproj = nc.dram_tensor("bproj", [1, D], F32, kind="ExternalInput").ap()
    wqk = nc.dram_tensor("wqk", [D, 32], F32, kind="ExternalInput").ap()
    wvr = nc.dram_tensor("wvr", [D, D], F32, kind="ExternalInput").ap()
    wor = nc.dram_tensor("wor", [D, D], F32, kind="ExternalInput").ap()
    wg = nc.dram_tensor("wg", [D, E], F32, kind="ExternalInput").ap()
    w1s = nc.dram_tensor("w1s", [2, D, F_FF], F32, kind="ExternalInput").ap()
    w2s = nc.dram_tensor("w2s", [2, F_FF, D], F32, kind="ExternalInput").ap()
    coremask = nc.dram_tensor("coremask", [8, 1], F32, kind="ExternalInput").ap()
    corec = nc.dram_tensor("corec", [128, 1], F32, kind="ExternalInput").ap()

    ln_dram = nc.dram_tensor("ln_dram", [T, D], BF16)
    ao_dram = nc.dram_tensor("ao_dram", [T, D], BF16)
    lnall_c = [nc.dram_tensor(f"lnall{i}", [8 * T, D], BF16) for i in range(2)]
    counts_c = [nc.dram_tensor(f"counts_d{i}", [1, E], F32) for i in range(2)]
    agcounts_c = [nc.dram_tensor(f"agcounts_d{i}", [8, E], F32) for i in range(2)]
    idtab_c = [nc.dram_tensor(f"idtab{i}", [E * CAP + 128, 1], F32) for i in range(2)]
    idall_c = [nc.dram_tensor(f"idall{i}", [E * CAP + 128, 1], F32) for i in range(2)]
    xe_c = [nc.dram_tensor(f"xe_dram{i}", [2 * CAP, D], BF16) for i in range(2)]
    eo_c = [nc.dram_tensor(f"eo_dram{i}", [2 * CAP, D], BF16) for i in range(2)]
    expall_c = [nc.dram_tensor(f"expall{i}", [E * CAP, D], BF16) for i in range(2)]

    def c_bf(a):
        return np.asarray(a, np.float32).astype(BF)

    stair = np.zeros((128, 31), np.float32)
    stair[:, 15] = 1.0
    rs = np.zeros((16, 16 * 128), np.float32)
    for i in range(16):
        rs[i, i * 128:(i + 1) * 128] = 1.0
    jl, tl = np.meshgrid(np.arange(128), np.arange(128), indexing="ij")
    consts_np = {
        "tri128i_b": c_bf(np.triu(np.ones((128, 128)))),
        "tri128e_f": np.triu(np.ones((128, 128), np.float32), 1),
        "tri128e_b": c_bf(np.triu(np.ones((128, 128)), 1)),
        "tri16e_b": c_bf(np.triu(np.ones((16, 16)), 1)),
        "tri16e_f": np.triu(np.ones((16, 16), np.float32), 1),
        "stair_b": c_bf(stair),
        "stair_f": stair,
        "rs_b": c_bf(rs),
        "rs_f": rs,
        "ones16c_f": np.ones((16, 1), np.float32),
        "ones1x128_b": c_bf(np.ones((1, 128))),
        "ones1x128_f": np.ones((1, 128), np.float32),
        "ones1xT_b": c_bf(np.ones((1, 1024))),
        "onescol_b": c_bf(np.ones((128, 1))),
        "iota16_f": np.broadcast_to(np.arange(16, dtype=np.float32), (128, 16)).copy(),
        "pidx_f": np.arange(128, dtype=np.float32).reshape(128, 1),
        "mask_i": c_bf((jl == tl).astype(np.float32)),
        "mask_prev": c_bf(np.isin(jl - tl, [0, 64, 96, 112, 120, 124, 126, 127]).astype(np.float32)),
        "mask_cur": c_bf(np.isin(tl - jl, [0, 1, 2, 4, 8, 16, 32, 64]).astype(np.float32)),
    }
    const_dram = {k: nc.inline_tensor(v, name="c_" + k) for k, v in consts_np.items()}

    sb = nc.alloc_sbuf_tensor
    xsb = sb("xsb", [128, NT, D], F32)
    C = {}
    for k, v in consts_np.items():
        if k in ("rs_f", "tri128e_f"):
            continue
        C[k] = sb("s_" + k, list(v.shape), BF16 if v.dtype == BF else F32)
    scal = sb("scal", [128, 8], F32)
    bqkv_b = sb("bqkv_b", [1, 3 * D], BF16)
    bproj_b = sb("bproj_b", [1, D], BF16)
    wqk_b = sb("wqk_b", [128, 8, 32], BF16)
    wg_b = sb("wg_b", [128, 8, E], BF16)
    cmask_s = sb("cmask_s", [8, 1], F32)
    corec_s = sb("corec_s", [128, 1], F32)
    idxf_s = sb("idxf_s", [128, NT], F32)
    gsv_s = sb("gsv_s", [128, NT], F32)
    slot2_s = sb("slot2_s", [128, NT], I32)
    oneh_s = sb("oneh_s", [128, NT, E], F32)

    for k in C:
        p.op("sync", dma(C[k][:], const_dram[k].ap()[:, :]), incs=["ldc"], dma=True)
    p.op("sync", dma(cmask_s[:], coremask[:, :]), incs=["ldc"], dma=True)
    p.op("sync", dma(corec_s[:], corec[:, :]), incs=["ldc"], dma=True)
    p.op("gpsimd", dma(bqkv_b[:], bqkv[:, :]), incs=["ldg"], dma=True)
    p.op("gpsimd", dma(bproj_b[:], bproj[:, :]), incs=["ldg"], dma=True)
    p.op("gpsimd", dma(wqk_b[:], wqk.rearrange("(a p) c -> p a c", p=128)),
         incs=["ldg"], dma=True)
    t_ldg = p.op("gpsimd", dma(wg_b[:], wg.rearrange("(a p) c -> p a c", p=128)),
                 incs=["ldg"], dma=True)
    t_ldc = ("sync:ldc", p.sems["sync:ldc"][1])

    p.op("vector", vop("memset", scal[:, 0:1], 1e-5))
    p.op("vector", vop("memset", scal[:, 1:2], float(np.pi / 2)))
    p.op("vector", vop("memset", scal[:, 2:3], 1e-6))
    tk_scal = p.token("vector", "scal")

    tk_x = []
    for i in range(NT):
        tk_x.append(p.op("sync", dma(xsb[:, i, :], x_in[i * 128:(i + 1) * 128, :]),
                         incs=["ldx"], dma=True))

    psc = [0]
    sbc = [0]

    def mk_psum(stack, shape, dt=F32):
        psc[0] += 1
        return stack.enter_context(nc.psum_tensor(f"ps{psc[0]}", shape, dt))

    def mk_sb(stack, shape, dt):
        sbc[0] += 1
        return stack.enter_context(nc.sbuf_tensor(f"sb{sbc[0]}", shape, dt))

    def mm(out, lhsT, rhs, start, stop):
        def fn(eng):
            n = rhs.shape[-1]
            if n <= 512:
                return eng.matmul(out, lhsT, rhs, start=start, stop=stop)
            inst = None
            for c0 in range(0, n, 512):
                c1 = min(c0 + 512, n)
                inst = eng.matmul(out[..., c0:c1], lhsT, rhs[..., c0:c1],
                                  start=start, stop=stop)
            return inst
        return fn

    def ln_pass(stack, extra_wait=()):
        lnb = mk_sb(stack, [128, 2, D], BF16)
        sqs = mk_sb(stack, [128, 1, D], BF16)
        st8 = mk_sb(stack, [128, 2, 8], F32)
        toks = []
        prev = [None, None]
        for i in range(NT):
            b = i % 2
            w = list(extra_wait) + [tk_scal, prev[b]]
            t_sq = p.op("scalar", vop("activation", sqs[:, 0, :], xsb[:, i, :], AF.Square,
                                      accum_out=st8[:, b, 1:2]), waits=w, incs=["lnsq"])
            p.op("vector", vop("reduce_sum", st8[:, b, 0:1], xsb[:, i, :], axis=AX.X),
                 waits=w)
            p.op("vector", vop("tensor_scalar", out=st8[:, b, 2:3], in0=st8[:, b, 0:1],
                               scalar1=1.0 / D, scalar2=None, op0=ALU.mult), waits=[t_sq])
            p.op("vector", vop("tensor_scalar", out=st8[:, b, 3:4], in0=st8[:, b, 2:3],
                               scalar1=-1.0, scalar2=None, op0=ALU.mult))
            p.op("vector", vop("tensor_tensor", out=st8[:, b, 4:5], in0=st8[:, b, 2:3],
                               in1=st8[:, b, 2:3], op=ALU.mult))
            p.op("vector", vop("tensor_scalar", out=st8[:, b, 5:6], in0=st8[:, b, 1:2],
                               scalar1=1.0 / D, scalar2=None, op0=ALU.mult))
            p.op("vector", vop("tensor_tensor", out=st8[:, b, 4:5], in0=st8[:, b, 5:6],
                               in1=st8[:, b, 4:5], op=ALU.subtract))
            t_var = p.token("vector", "lnv")
            t_sd = p.op("scalar", vop("activation", st8[:, b, 6:7], st8[:, b, 4:5],
                                      AF.Sqrt, bias=scal[:, 0:1]), waits=[t_var], incs=["lnsd"])
            p.op("vector", vop("reciprocal", st8[:, b, 7:8], st8[:, b, 6:7]), waits=[t_sd])
            t_r = p.token("vector", "lnv")
            p.op("vector", vop("tensor_scalar", out=lnb[:, b, :], in0=xsb[:, i, :],
                               scalar1=st8[:, b, 3:4], scalar2=st8[:, b, 7:8],
                               op0=ALU.add, op1=ALU.mult), waits=[t_r])
            t_ln = p.token("vector", "lnv")
            tst = p.op("sync", dma(ln_dram.ap()[i * 128:(i + 1) * 128, :], lnb[:, b, :]),
                       waits=[t_ln], incs=[f"lnst{b}"], dma=True)
            prev[b] = tst
            toks.append(tst)
        return toks

    def load_T_into(bufT, src_dram, wait_toks, rows=None):
        toks = []
        for F_ in range(8):
            srcap = src_dram.ap() if rows is None else src_dram.ap()[rows[0]:rows[1], :]
            toks.append(p.op("sync", dmat(bufT[:, F_, :],
                                          srcap[:, F_ * 128:(F_ + 1) * 128]),
                             waits=wait_toks, incs=["ldT"], dma=True))
        return toks

    def load_T(stack, src_dram, wait_toks, width):
        bufT = mk_sb(stack, [128, 8, width], BF16)
        return bufT, load_T_into(bufT, src_dram, wait_toks)

    def out_proj(stack, w_dram, bias_sb, src_toks, add_bias):
        aoT, aoT_toks = load_T(stack, ao_dram, src_toks, T)
        wpr = mk_sb(stack, [128, 8, D], BF16)
        t_w = p.op("gpsimd", dma(wpr[:], w_dram.rearrange("(a p) c -> p a c", p=128)),
                   waits=[src_toks[-1]], incs=["wprld"], dma=True)
        ps_pr = mk_psum(stack, [128, 2, D], F32)
        prev = [None, None]
        for i in range(NT):
            b = i % 2
            w = ([t_w] + aoT_toks if i < 2 else []) + [prev[b]]
            for dc in range(8):
                p.op("tensor", mm(ps_pr[:, b, :], aoT[:, dc, i * 128:(i + 1) * 128],
                                  wpr[:, dc, :], start=(dc == 0),
                                  stop=(dc == 7 and not add_bias)),
                     waits=w if dc == 0 else [])
            if add_bias:
                p.op("tensor", mm(ps_pr[:, b, :], C["ones1x128_b"][:], bias_sb[:, :],
                                  start=False, stop=True))
            t_mm = p.token("tensor", "prmm")
            p.op("vector", vop("tensor_tensor", out=xsb[:, i, :], in0=xsb[:, i, :],
                               in1=ps_pr[:, b, :], op=ALU.add), waits=[t_mm])
            prev[b] = p.token("vector", "pradd")

    def sublayer_attn_a():
        stack = contextlib.ExitStack()
        with stack:
            lnT = mk_sb(stack, [128, 8, T], BF16)
            ln_stack = contextlib.ExitStack()
            ln_toks = ln_pass(ln_stack, extra_wait=[tk_x[-1], t_ldc, t_ldg])
            lnT_toks = load_T_into(lnT, ln_dram, ln_toks)
            ln_stack.close()
            p.barrier("a1ln")
            if STAGE[0] < 2:
                return

            ao_toks = []
            mask_of = {8: "mask_i", 4: "mask_i", 2: "mask_i", 1: "mask_prev", 0: "mask_cur"}
            for hg in range(2):
                hstack = contextlib.ExitStack()
                with hstack:
                    qP = mk_sb(hstack, [128, 4, T], BF16)
                    kz = mk_sb(hstack, [128, 8, T], BF16)
                    wch = mk_sb(hstack, [128, 2, 8, 128], BF16)
                    p.op("vector", vop("memset", kz[:], 0.0))
                    t_kz0 = p.token("vector")
                    qk_stack = contextlib.ExitStack()
                    pss = [mk_psum(qk_stack, [128, 512], F32),
                           mk_psum(qk_stack, [128, 512], F32)]
                    ev = [None, None]
                    for fcl in range(8):
                        wb = fcl % 2
                        col0 = hg * 512 + fcl * 128 if fcl < 4 else 1024 + hg * 512 + (fcl - 4) * 128
                        t_wl = p.op("gpsimd", dma(
                            wch[:, wb, :, :],
                            wqkv.rearrange("(a p) c -> p a c", p=128)[:, :, col0:col0 + 128]),
                            waits=[ev[0], ev[1]] if fcl >= 2 else [], incs=[f"wqkl{wb}"], dma=True)
                        for g in range(4):
                            pb = (fcl * 4 + g) % 2
                            w = [t_wl, ev[pb]] + (lnT_toks if fcl == 0 and g == 0 else [])
                            for dc in range(8):
                                p.op("tensor", mm(pss[pb][:], wch[:, wb, dc, :],
                                                  lnT[:, dc, g * 512:(g + 1) * 512],
                                                  start=(dc == 0), stop=False),
                                     waits=w if dc == 0 else [])
                            p.op("tensor", mm(pss[pb][:], bqkv_b[0:1, col0:col0 + 128],
                                              C["ones1xT_b"][0:1, 0:512], start=False, stop=True))
                            t_mm = p.token("tensor", "qkmm")
                            if fcl < 4:
                                ev[pb] = p.op("scalar", vop("copy",
                                                            out=qP[:, fcl, g * 512:(g + 1) * 512],
                                                            in_=pss[pb][:]),
                                              waits=[t_mm], incs=["qkev"])
                            else:
                                p.op("scalar", vop("copy",
                                                   out=kz[0:64, (fcl - 4) * 2, g * 512:(g + 1) * 512],
                                                   in_=pss[pb][0:64, :]),
                                     waits=[t_mm, t_kz0])
                                ev[pb] = p.op("scalar", vop("copy",
                                                            out=kz[64:128, (fcl - 4) * 2 + 1, g * 512:(g + 1) * 512],
                                                            in_=pss[pb][64:128, :]),
                                              incs=["qkev"])
                    last_qk = p.token("scalar")
                    qk_stack.close()
                    if STAGE[0] < 3:
                        p.barrier(f"dbg{hg}")
                        continue

                    v1h = mk_sb(hstack, [128, 10, 512], BF16)
                    vch = mk_sb(hstack, [128, 8, 512], BF16)
                    em = mk_sb(hstack, [128, 5, 8, 128], BF16)
                    aob = mk_sb(hstack, [128, 2, 512], BF16)
                    zr = mk_sb(hstack, [128, 2, 8], F32)
                    ps_v = mk_psum(hstack, [128, 2, 512], F32)
                    ps_sc = mk_psum(hstack, [128, 2, 512], F32)
                    ps_av = mk_psum(hstack, [128, 2, 512], F32)
                    ps_z = mk_psum(hstack, [128, 2, 8], F32)
                    t_wv = p.op("gpsimd", dma(
                        vch[:],
                        wqkv.rearrange("(a p) c -> p a c", p=128)[:, :, 2048 + hg * 512:2048 + (hg + 1) * 512]),
                        incs=[f"wvl{hg}"], dma=True)
                    v_done = [None] * NT
                    slot_reader = [None] * 10
                    ev_v = [None, None]
                    sc_ev = [None, None]
                    av_ev = [None, None]
                    msk_prev = None
                    for i in range(NT):
                        b = i % 2
                        sl = i % 10
                        w = ([t_wv] + lnT_toks + [last_qk] if i < 2 else []) + \
                            [ev_v[b], slot_reader[sl]]
                        for dc in range(8):
                            p.op("tensor", mm(ps_v[:, b, :],
                                              lnT[:, dc, i * 128:(i + 1) * 128],
                                              vch[:, dc, :], start=(dc == 0), stop=False),
                                 waits=w if dc == 0 else [])
                        p.op("tensor", mm(ps_v[:, b, :], C["ones1x128_b"][:],
                                          bqkv_b[0:1, 2048 + hg * 512:2048 + (hg + 1) * 512],
                                          start=False, stop=True))
                        t_vm = p.token("tensor", "vmm")
                        tve = p.op("scalar", vop("copy", out=v1h[:, sl, :], in_=ps_v[:, b, :]),
                                   waits=[t_vm], incs=["vev"])
                        ev_v[b] = tve
                        v_done[i] = tve
                        if STAGE[0] < 4:
                            slot_reader[sl] = tve
                            continue
                        if STAGE[0] == 42:
                            pass

                        chunks = [(ci, off) for ci, off in enumerate(SHIFT_OFFS) if i - off >= 0]
                        pairs = [(ci, off, h) for (ci, off) in chunks for h in range(8)]
                        em_groups = []
                        for gi in range(0, len(pairs), 4):
                            grp = pairs[gi:gi + 4]
                            pb = (gi // 4) % 2
                            w = [sc_ev[pb], msk_prev]
                            for k_, (ci, off, h) in enumerate(grp):
                                c = i - off
                                p.op("tensor", mm(ps_sc[:, pb, k_ * 128:(k_ + 1) * 128],
                                                  kz[:, h, c * 128:(c + 1) * 128],
                                                  qP[:, h // 2, i * 128:(i + 1) * 128],
                                                  start=True, stop=True),
                                     waits=w if k_ == 0 else [])
                            t_g = p.token("tensor", "scmm")
                            base = grp[0][0] * 8 + grp[0][2]
                            if STAGE[0] == 41:
                                t_e = p.op("scalar", vop(
                                    "copy",
                                    out=em.rearrange("p c h f -> p (c h) f")[:, base:base + len(grp), :],
                                    in_=ps_sc[:, pb, 0:len(grp) * 128]),
                                    waits=[t_g], incs=["scexp"])
                            else:
                                t_e = p.op("scalar", vop(
                                    "activation",
                                    em.rearrange("p c h f -> p (c h) f")[:, base:base + len(grp), :],
                                    ps_sc.rearrange("p b (k f) -> p b k f", f=128)[:, pb, 0:len(grp), :],
                                    AF.Exp, scale=0.125),
                                    waits=[t_g], incs=["scexp"])
                            sc_ev[pb] = t_e
                            em_groups.append((t_e, grp))
                        if STAGE[0] < 50:
                            msk_prev = sc_ev[0]
                            av_ev[b] = em_groups[-1][0]
                            continue
                        mk_toks = []
                        for (t_e, grp) in em_groups:
                            runs = []
                            for (ci, off, h) in grp:
                                if runs and runs[-1][0] == ci:
                                    runs[-1][2] += 1
                                else:
                                    runs.append([ci, h, 1])
                            for (ci, h0, n) in runs:
                                off = SHIFT_OFFS[ci]
                                p.op("vector", vop("tensor_tensor",
                                                   out=em[:, ci, h0:h0 + n, :],
                                                   in0=em[:, ci, h0:h0 + n, :],
                                                   in1=C[mask_of[off]][:, None, :].broadcast_to([128, n, 128]),
                                                   op=ALU.mult), waits=[t_e])
                            mk_toks.append(p.token("vector", "msk"))
                        msk_prev = mk_toks[-1]
                        w_av = [av_ev[b]] + mk_toks + [v_done[i - off] for (ci, off) in chunks]
                        first = True
                        for h in range(8):
                            for (ci, off) in chunks:
                                p.op("tensor", mm(ps_av[:, b, h * 64:(h + 1) * 64],
                                                  em[:, ci, h, :],
                                                  v1h[:, (i - off) % 10, h * 64:(h + 1) * 64],
                                                  start=(ci == chunks[0][0]),
                                                  stop=(ci == chunks[-1][0])),
                                     waits=w_av if first else [])
                                first = False
                            for (ci, off) in chunks:
                                p.op("tensor", mm(ps_z[:, b, h:h + 1], em[:, ci, h, :],
                                                  C["onescol_b"][:],
                                                  start=(ci == chunks[0][0]),
                                                  stop=(ci == chunks[-1][0])))
                        t_av = p.token("tensor", "avmm")
                        for (ci, off) in chunks:
                            slot_reader[(i - off) % 10] = t_av
                        t_zr = p.op("vector", vop("reciprocal", zr[:, b, :], ps_z[:, b, :]),
                                    waits=[t_av], incs=["zrec"])
                        p.op("vector", vop("tensor_tensor",
                                           out=aob.rearrange("p b (h e) -> p b h e", e=64)[:, b],
                                           in0=ps_av.rearrange("p b (h e) -> p b h e", e=64)[:, b],
                                           in1=zr[:, b, :, None].broadcast_to([128, 8, 64]),
                                           op=ALU.mult), waits=[t_zr])
                        t_ao = p.token("vector", "aon")
                        t_aos = p.op("sync", dma(
                            ao_dram.ap()[i * 128:(i + 1) * 128, hg * 512:(hg + 1) * 512],
                            aob[:, b, :]), waits=[t_ao], incs=[f"aost{b}"], dma=True)
                        av_ev[b] = t_aos
                        ao_toks.append(t_aos)
                p.barrier(f"a1h{hg}")
        if STAGE[0] < 6 or not ao_toks:
            return
        stack = contextlib.ExitStack()
        with stack:
            out_proj(stack, wproj, bproj_b, ao_toks, add_bias=True)
        p.barrier("a1")

    def sublayer_moe(call):
        lnall = lnall_c[call]
        counts_d = counts_c[call]
        agcounts_d = agcounts_c[call]
        idtab = idtab_c[call]
        idall = idall_c[call]
        xe_dram = xe_c[call]
        eo_dram = eo_c[call]
        expall = expall_c[call]
        stack = contextlib.ExitStack()
        with stack:
            rs_f = mk_sb(stack, [16, 16 * 128], F32)
            tri128e_f = mk_sb(stack, [128, 128], F32)
            p.op("sync", dma(rs_f[:], const_dram["rs_f"].ap()[:, :]), incs=["ldT"], dma=True)
            t_mc = p.op("sync", dma(tri128e_f[:], const_dram["tri128e_f"].ap()[:, :]),
                        incs=["ldT"], dma=True)
            ln_toks = ln_pass(stack)
            t_ag = p.op("gpsimd", collective("AllGather", ALU.bypass,
                                             [ln_dram.ap().opt()], [lnall.ap().opt()]),
                        waits=ln_toks, incs=[f"agd{call}"], cc=True)
            lnT, lnT_toks = load_T(stack, ln_dram, ln_toks, T)

            ps_lg = mk_psum(stack, [128, 2, E], F32)
            g8 = mk_sb(stack, [128, 2, 8], F32)
            gi8 = mk_sb(stack, [128, 2, 8], U32)
            gsc = mk_sb(stack, [128, 2, 4], F32)
            eg = mk_sb(stack, [128, 2, E], F32)
            lg_ev = [None, None]
            oneh_toks = []
            for i in range(NT):
                b = i % 2
                w = (lnT_toks + [t_ldg, t_ldc] if i < 2 else []) + [lg_ev[b]]
                for dc in range(8):
                    p.op("tensor", mm(ps_lg[:, b, :], lnT[:, dc, i * 128:(i + 1) * 128],
                                      wg_b[:, dc, :], start=(dc == 0), stop=(dc == 7)),
                         waits=w if dc == 0 else [])
                t_lg = p.token("tensor", "lgmm")
                t_m8 = p.op("vector", vop("max", g8[:, b, :], ps_lg[:, b, :]),
                            waits=[t_lg], incs=["gv"])
                p.op("vector", vop("max_index", gi8[:, b, :], g8[:, b, :], ps_lg[:, b, :]),
                     waits=[t_m8])
                p.op("vector", vop("tensor_copy", out=idxf_s[:, i:i + 1], in_=gi8[:, b, 0:1]))
                p.op("vector", vop("tensor_scalar", out=gsc[:, b, 0:1], in0=g8[:, b, 0:1],
                                   scalar1=-1.0, scalar2=None, op0=ALU.mult))
                t_nm = p.token("vector", "gv")
                t_ex = p.op("scalar", vop("activation", eg[:, b, :], ps_lg[:, b, :], AF.Exp,
                                          bias=gsc[:, b, 0:1], accum_out=gsc[:, b, 1:2]),
                            waits=[t_nm], incs=["gexp"])
                p.op("vector", vop("reciprocal", gsc[:, b, 2:3], gsc[:, b, 1:2]),
                     waits=[t_ex])
                p.op("vector", vop("tensor_copy", out=gsv_s[:, i:i + 1], in_=gsc[:, b, 2:3]))
                p.op("vector", vop("tensor_tensor", out=oneh_s[:, i, :],
                                   in0=C["iota16_f"][:],
                                   in1=idxf_s[:, i:i + 1].broadcast_to([128, E]),
                                   op=ALU.is_equal))
                t_oh = p.token("vector", "gv")
                lg_ev[b] = t_oh
                oneh_toks.append(t_oh)

            if MSTAGE[0] < 2:
                p.barrier(f"mdbg{call}")
                return
            ps_tm = mk_psum(stack, [16, E], F32)
            for i in range(NT):
                p.op("tensor", mm(ps_tm[:], C["stair_f"][:, 15 - i:31 - i], oneh_s[:, i, :],
                                  start=(i == 0), stop=(i == NT - 1)),
                     waits=[oneh_toks[i]])
            t_tm = p.token("tensor", "tmm")

            tmsb = mk_sb(stack, [16, E], F32)
            offm = mk_sb(stack, [16, E], F32)
            crow = mk_sb(stack, [1, 2, E], F32)
            bco = mk_sb(stack, [128, E], F32)
            agc = mk_sb(stack, [8, E], F32)
            t_te = p.op("vector", vop("tensor_copy", out=tmsb[:], in_=ps_tm[:]),
                        waits=[t_tm], incs=["gv"])
            ps_o16 = mk_psum(stack, [16, E], F32)
            t_om = p.op("tensor", mm(ps_o16[:], C["tri16e_f"][:], tmsb[:], True, True),
                        waits=[t_te], incs=["omm"])
            t_of = p.op("vector", vop("tensor_copy", out=offm[:], in_=ps_o16[:]),
                        waits=[t_om], incs=["gv"])
            offh = mk_sb(stack, [16, E], BF16)
            offl = mk_sb(stack, [16, E], BF16)
            p.op("vector", vop("tensor_copy", out=offh[:], in_=offm[:]))
            p.op("vector", vop("tensor_tensor", out=offl[:], in0=offm[:],
                               in1=offh[:], op=ALU.subtract))
            t_ofhl = p.token("vector")
            ps_c = mk_psum(stack, [1, 2, E], F32)
            t_cm = p.op("tensor", mm(ps_c[:, 0, :], C["ones16c_f"][:], tmsb[:], True, True),
                        waits=[t_te], incs=["cmm"])
            t_ce = p.op("vector", vop("tensor_copy", out=crow[:, 0, :], in_=ps_c[:, 0, :]),
                        waits=[t_cm], incs=["gv"])
            t_cst = p.op("sync", dma(counts_d.ap()[:, :], crow[:, 0, :]),
                         waits=[t_ce], incs=["cst"], dma=True)
            t_cag = p.op("gpsimd", collective("AllGather", ALU.bypass,
                                              [counts_d.ap().opt()], [agcounts_d.ap().opt()]),
                         waits=[t_cst], incs=[f"agc{call}"], cc=True)
            t_agl = p.op("sync", dma(agc[:], agcounts_d.ap()[:, :]),
                         waits=[t_cag], incs=["agl"], dma=True)
            t_co = p.op("tensor", mm(ps_c[:, 1, :], cmask_s[:], agc[:], True, True),
                        waits=[t_agl], incs=["comm"])
            t_coe = p.op("vector", vop("tensor_copy", out=crow[:, 1, :], in_=ps_c[:, 1, :]),
                         waits=[t_co], incs=["gv"])
            ps_bc = mk_psum(stack, [128, E], F32)
            t_bc = p.op("tensor", mm(ps_bc[:], C["ones1x128_f"][:], crow[:, 1, :], True, True),
                        waits=[t_coe], incs=["bcmm"])
            t_bce = p.op("vector", vop("tensor_copy", out=bco[:], in_=ps_bc[:]),
                         waits=[t_bc], incs=["gv"])

            if MSTAGE[0] < 3:
                p.barrier(f"mdbg{call}")
                return
            ps_rks = [mk_psum(stack, [128, E], F32), mk_psum(stack, [128, E], F32)]
            oneb = mk_sb(stack, [128, 2, E], BF16)
            lsc = mk_sb(stack, [128, 2, E], F32)
            l8 = mk_sb(stack, [128, 2, 8], F32)
            sl32 = mk_sb(stack, [128, 2, 1], I32)
            zt = mk_sb(stack, [128, 128], F32)
            dumpslot = mk_sb(stack, [128, 1], F32)
            p.op("vector", vop("memset", zt[:], 0.0))
            p.op("vector", vop("tensor_scalar", out=dumpslot[:], in0=C["pidx_f"][:],
                               scalar1=float(E * CAP), scalar2=None, op0=ALU.add))
            t_z = p.token("vector", "gv")
            t_zt0 = p.op("sync", dma(idtab.ap()[0:E * CAP].rearrange("(p a) c -> p (a c)", p=128),
                                     zt[:]), waits=[t_z], incs=["ztab"], dma=True)
            t_zt = p.op("sync", dma(idtab.ap()[E * CAP:E * CAP + 128].rearrange("(p a) c -> p (a c)", p=128),
                                    zt[:, 0:1]), waits=[t_z], incs=["ztab"], dma=True)
            rk_ev = [None, None]
            scat_toks = [t_zt]
            for i in range(NT):
                b = i % 2
                t_ob = p.op("vector", vop("tensor_copy", out=oneb[:, b, :],
                                          in_=oneh_s[:, i, :]), waits=[rk_ev[b]])
                w = [t_bce, t_ofhl, t_ob]
                p.op("tensor", mm(ps_rks[b][:], C["tri128e_b"][:], oneb[:, b, :],
                                  True, False), waits=w)
                p.op("tensor", mm(ps_rks[b][:], C["rs_b"][:, i * 128:(i + 1) * 128],
                                  offh[:], False, False))
                t_rk = p.op("tensor", mm(ps_rks[b][:], C["rs_b"][:, i * 128:(i + 1) * 128],
                                         offl[:], False, True), incs=["rkmm"])
                p.op("vector", vop("tensor_tensor", out=lsc[:, b, :], in0=ps_rks[b][:],
                                   in1=bco[:], op=ALU.add), waits=[t_rk])
                p.op("vector", vop("tensor_tensor", out=lsc[:, b, :], in0=lsc[:, b, :],
                                   in1=oneh_s[:, i, :], op=ALU.mult))
                p.op("vector", vop("reduce_sum", l8[:, b, 0:1], lsc[:, b, :], axis=AX.X))
                p.op("vector", vop("tensor_scalar", out=l8[:, b, 1:2], in0=l8[:, b, 0:1],
                                   scalar1=float(CAP), scalar2=None, op0=ALU.is_lt))
                p.op("vector", vop("tensor_scalar", out=l8[:, b, 2:3], in0=l8[:, b, 1:2],
                                   scalar1=-1.0, scalar2=-1.0, op0=ALU.add, op1=ALU.mult))
                p.op("vector", vop("tensor_tensor", out=l8[:, b, 2:3], in0=l8[:, b, 2:3],
                                   in1=dumpslot[:], op=ALU.mult))
                p.op("vector", vop("tensor_scalar", out=l8[:, b, 3:4], in0=idxf_s[:, i:i + 1],
                                   scalar1=float(CAP), scalar2=None, op0=ALU.mult))
                p.op("vector", vop("tensor_tensor", out=l8[:, b, 4:5], in0=l8[:, b, 3:4],
                                   in1=l8[:, b, 0:1], op=ALU.add))
                p.op("vector", vop("tensor_tensor", out=l8[:, b, 4:5], in0=l8[:, b, 4:5],
                                   in1=l8[:, b, 1:2], op=ALU.mult))
                p.op("vector", vop("tensor_tensor", out=l8[:, b, 4:5], in0=l8[:, b, 4:5],
                                   in1=l8[:, b, 2:3], op=ALU.add))
                if MSTAGE[0] == 312:
                    rk_ev[b] = p.token("vector")
                    scat_toks.append(rk_ev[b])
                    continue
                p.op("vector", vop("tensor_copy", out=sl32[:, b, :], in_=l8[:, b, 4:5]))
                p.op("vector", vop("tensor_scalar", out=l8[:, b, 5:6], in0=l8[:, b, 0:1],
                                   scalar1=float(CAP - 1), scalar2=None, op0=ALU.min))
                p.op("vector", vop("tensor_tensor", out=l8[:, b, 5:6], in0=l8[:, b, 5:6],
                                   in1=l8[:, b, 3:4], op=ALU.add))
                p.op("vector", vop("tensor_copy", out=slot2_s[:, i:i + 1], in_=l8[:, b, 5:6]))
                p.op("vector", vop("tensor_tensor", out=gsv_s[:, i:i + 1],
                                   in0=gsv_s[:, i:i + 1], in1=l8[:, b, 1:2], op=ALU.mult))
                p.op("vector", vop("tensor_tensor", out=l8[:, b, 6:7], in0=C["pidx_f"][:],
                                   in1=corec_s[:], op=ALU.add))
                p.op("vector", vop("tensor_scalar", out=l8[:, b, 6:7], in0=l8[:, b, 6:7],
                                   scalar1=float(i * 128), scalar2=None, op0=ALU.add))
                t_v = p.token("vector", "gv")
                if MSTAGE[0] >= 32:
                    t_s = p.op("gpsimd", iscatter(idtab.ap()[:, :], sl32[:, b, :],
                                                  l8[:, b, 6:7]),
                               waits=[t_v, t_zt, scat_toks[-1] if len(scat_toks) > 1 else None],
                               incs=["scat"], dma=True)
                else:
                    t_s = t_v
                rk_ev[b] = t_s
                scat_toks.append(t_s)
            if MSTAGE[0] < 33:
                p.barrier(f"mdbg{call}")
                return
            t_ar = p.op("gpsimd", collective("AllReduce", ALU.add,
                                             [idtab.ap().opt()], [idall.ap().opt()]),
                        waits=scat_toks, incs=[f"arid{call}"], cc=True)

            if MSTAGE[0] < 4:
                p.barrier(f"mdbg{call}")
                return
            gidb = mk_sb(stack, [128, 16], F32)
            gid32 = mk_sb(stack, [128, 16], I32)
            idv = mk_sb(stack, [128, 16], F32)
            idv32 = mk_sb(stack, [128, 16], I32)
            for k_ in range(16):
                p.op("vector", vop("tensor_scalar", out=gidb[:, k_:k_ + 1], in0=corec_s[:],
                                   scalar1=float(k_ * 128), scalar2=None, op0=ALU.add))
                p.op("vector", vop("tensor_tensor", out=gidb[:, k_:k_ + 1],
                                   in0=gidb[:, k_:k_ + 1], in1=C["pidx_f"][:], op=ALU.add))
            p.op("vector", vop("tensor_copy", out=gid32[:], in_=gidb[:]))
            t_gid = p.token("vector", "gv")
            idl_toks = []
            for k_ in range(16):
                idl_toks.append(p.op("gpsimd", igather(idv[:, k_:k_ + 1], idall.ap()[:, :],
                                                       gid32[:, k_:k_ + 1]),
                                     waits=[t_ar, t_gid], incs=["idl"], dma=True))
            t_iv = p.op("vector", vop("tensor_copy", out=idv32[:], in_=idv[:]),
                        waits=[idl_toks[-1]], incs=["gv"])
            gath_toks = []
            xeb = mk_sb(stack, [128, 2, D], BF16)
            xeb_prev = [None, None]
            for k_ in range(16):
                b_ = k_ % 2
                t_gx = p.op("gpsimd", igather(xeb[:, b_, :], lnall.ap()[:, :],
                                              idv32[:, k_:k_ + 1]),
                            waits=[t_iv, t_ag, xeb_prev[b_]], incs=[f"xeg{b_}"], dma=True)
                t_sx = p.op("sync", dma(xe_dram.ap()[k_ * 128:(k_ + 1) * 128, :],
                                        xeb[:, b_, :]),
                            waits=[t_gx], incs=[f"xest{b_}"], dma=True)
                xeb_prev[b_] = t_sx
                gath_toks.append(t_sx)

        if MSTAGE[0] < 5:
            p.barrier(f"mdbg{call}")
            return
        p.barrier(f"moepre{call}")
        stack = contextlib.ExitStack()
        with stack:
            if MSTAGE[0] in (71, 72, 73):
                safei = mk_sb(stack, [128, 1], I32)
                dummy = mk_sb(stack, [128, D], BF16)
                p.op("vector", vop("tensor_copy", out=safei[:], in_=C["pidx_f"][:]))
                t_si = p.token("vector")
            if MSTAGE[0] == 71:
                p.op("gpsimd", igather(dummy[:], lnall.ap()[:, :], safei[:]),
                     waits=[t_si], incs=["cbg0"], dma=True)
                p.barrier(f"mdbg{call}")
                return
            xeT = mk_sb(stack, [128, 8, CAP], BF16)
            xeT_toks0 = []
            if MSTAGE[0] == 72:
                for F_ in range(8):
                    xeT_toks0.append(p.op("sync", dmat(
                        xeT[:, F_, :],
                        xe_dram.ap()[0:CAP, F_ * 128:(F_ + 1) * 128]),
                        waits=gath_toks, incs=["xeT"], dma=True))
            if MSTAGE[0] == 72:
                p.op("gpsimd", igather(dummy[:], lnall.ap()[:, :], safei[:]),
                     waits=[t_si] + xeT_toks0, incs=["cbg0"], dma=True)
                p.barrier(f"mdbg{call}")
                return
            if MSTAGE[0] in (73, 74, 75, 76):
                safei73 = mk_sb(stack, [128, 1], I32)
                dummy73 = mk_sb(stack, [128, D], BF16)
                p.op("vector", vop("tensor_copy", out=safei73[:], in_=C["pidx_f"][:]))
                t_si73 = p.token("vector")
            hT = mk_sb(stack, [128, 32, 512], BF16)
            w1c = mk_sb(stack, [128, 2, 8, 512], BF16)
            w2c = mk_sb(stack, [128, 2, 4, 1024], BF16)
            eob = mk_sb(stack, [128, 2, 1024], BF16)
            eo_toks = []
            o_ev_g = None
            t_lastmm1 = None
            prev_relu = None
            w2_mm_tok = {}
            prev_ost = [None, None]
            for e_ in range(2):
                xeT_toks = []
                for F_ in range(8):
                    xeT_toks.append(p.op("sync", dmat(
                        xeT[:, F_, :],
                        xe_dram.ap()[e_ * CAP:(e_ + 1) * CAP, F_ * 128:(F_ + 1) * 128]),
                        waits=gath_toks + ([t_lastmm1] if e_ else []),
                        incs=["xeT"], dma=True))
                for th in range(2):
                    ph_stack = contextlib.ExitStack()
                    ps_h = mk_psum(ph_stack, [128, 2, 512], F32)
                    h_ev = [None, None]
                    w1_t = [None, None]
                    relu_done = []
                    for slab in range(8):
                        sb_ = slab % 2
                        w1_t[sb_] = p.op("gpsimd", dma(
                            w1c[:, sb_, :, :],
                            w1s[e_].rearrange("(a p) c -> p a c", p=128)[:, :, slab * 512:(slab + 1) * 512]),
                            waits=[relu_done[-5] if len(relu_done) >= 5 else t_lastmm1],
                            incs=[f"w1l{sb_}"], dma=True)
                        for fq in range(4):
                            fc = slab * 4 + fq
                            b = fc % 2
                            w = [w1_t[sb_], h_ev[b]] + \
                                (xeT_toks + [prev_relu, o_ev_g] if fc == 0 else [])
                            for dc in range(8):
                                p.op("tensor", mm(ps_h[:, b, :],
                                                  w1c[:, sb_, dc, fq * 128:(fq + 1) * 128],
                                                  xeT[:, dc, th * 512:(th + 1) * 512],
                                                  start=(dc == 0), stop=(dc == 7)),
                                     waits=w if dc == 0 else [])
                            t_h = p.token("tensor", "hmm")
                            t_lastmm1 = t_h
                            t_r = p.op("scalar", vop("activation", hT[:, fc, :],
                                                     ps_h[:, b, :], AF.Relu),
                                       waits=[t_h], incs=["relu"])
                            h_ev[b] = t_r
                            relu_done.append(t_r)
                    prev_relu = relu_done[-1]
                    ph_stack.close()
                    if MSTAGE[0] == 73 and e_ == 0 and th == 0:
                        p.op("gpsimd", igather(dummy73[:], lnall.ap()[:, :], safei73[:]),
                             waits=[t_si73, relu_done[-1]], incs=["cbg0"], dma=True)
                        p.barrier(f"mdbg{call}")
                        return
                    po_stack = contextlib.ExitStack()
                    ps_o = [mk_psum(po_stack, [128, 1024], F32) for _ in range(4)]
                    w2_t = [None, None]
                    ost_toks = []
                    for fc in range(32):
                        if fc % 4 == 0:
                            slab = fc // 4
                            wkey = (slab - 2) % 2 if slab >= 2 else None
                            w2_t[slab % 2] = p.op("gpsimd", dma(
                                w2c[:, slab % 2, :, :],
                                w2s[e_].rearrange("(a p) c -> p a c", p=128)[:, slab * 4:(slab + 1) * 4, :]),
                                waits=[w2_mm_tok.get(slab - 2) if slab >= 2 else o_ev_g],
                                incs=[f"w2l{slab % 2}"], dma=True)
                        for tq in range(4):
                            w = []
                            if fc == 0:
                                w = [relu_done[-1], o_ev_g]
                            if fc % 4 == 0 and tq == 0:
                                w = w + [w2_t[(fc // 4) % 2]]
                            p.op("tensor", mm(ps_o[tq][:],
                                              hT[:, fc, tq * 128:(tq + 1) * 128],
                                              w2c[:, (fc // 4) % 2, fc % 4, :],
                                              start=(fc == 0), stop=(fc == 31)),
                                 waits=w)
                        if fc % 4 == 3 and fc != 31:
                            w2_mm_tok[fc // 4] = p.token("tensor", "w2use")
                    t_o = p.token("tensor", "omm2")
                    for tq in range(4):
                        b = tq % 2
                        t_oe = p.op("vector", vop("tensor_copy", out=eob[:, b, :],
                                                  in_=ps_o[tq][:]),
                                    waits=[t_o, prev_ost[b]],
                                    incs=["oev"])
                        row0 = (e_ * 8 + th * 4 + tq) * 128
                        t_os = p.op("sync", dma(eo_dram.ap()[row0:row0 + 128, :],
                                                eob[:, b, :]),
                                    waits=[t_oe], incs=[f"ost{b}"], dma=True)
                        ost_toks.append(t_os)
                        prev_ost[b] = t_os
                        eo_toks.append(t_os)
                    o_ev_g = ost_toks[-1]
                    po_stack.close()
                    if MSTAGE[0] == 74 and e_ == 0 and th == 0:
                        p.op("gpsimd", igather(dummy73[:], lnall.ap()[:, :], safei73[:]),
                             waits=[t_si73, o_ev_g], incs=["cbg0"], dma=True)
                        p.barrier(f"mdbg{call}")
                        return
            if MSTAGE[0] == 75:
                p.op("gpsimd", igather(dummy73[:], lnall.ap()[:, :], safei73[:]),
                     waits=[t_si73] + eo_toks, incs=["cbg0"], dma=True)
                p.barrier(f"mdbg{call}")
                return
            t_eag = p.op("gpsimd", collective("AllGather", ALU.bypass,
                                              [eo_dram.ap().opt()], [expall.ap().opt()]),
                         waits=eo_toks, incs=[f"eag{call}"], cc=True)
            if MSTAGE[0] == 76:
                p.op("gpsimd", igather(dummy73[:], lnall.ap()[:, :], safei73[:]),
                     waits=[t_si73, t_eag], incs=["cbg0"], dma=True)
                p.barrier(f"mdbg{call}")
                return

            if MSTAGE[0] < 6:
                p.barrier(f"mdbg{call}")
                return
            got = mk_sb(stack, [128, 2, D], BF16)
            cb_ev = [None, None]
            if MSTAGE[0] in (65, 66, 67):
                safei = mk_sb(stack, [128, 1], I32)
                p.op("vector", vop("tensor_copy", out=safei[:], in_=C["pidx_f"][:]))
                t_si = p.token("vector")
                srct = {65: expall, 66: lnall, 67: eo_dram}[MSTAGE[0]]
                t_g = p.op("gpsimd", igather(got[:, 0, :], srct.ap()[:, :], safei[:]),
                           waits=[t_eag, t_si], incs=["cbg0"], dma=True)
                p.barrier(f"mdbg{call}")
                return
            nloop = 1 if MSTAGE[0] == 62 else NT
            srctab = lnall if MSTAGE[0] == 63 else expall
            for i in range(nloop):
                b = i % 2
                gw = [cb_ev[b]] if MSTAGE[0] == 64 else [t_eag, cb_ev[b]]
                t_g = p.op("gpsimd", igather(got[:, b, :], srctab.ap()[:, :],
                                             slot2_s[:, i:i + 1]),
                           waits=gw, incs=[f"cbg{b}"], dma=True)
                if MSTAGE[0] == 61:
                    cb_ev[b] = t_g
                    continue
                p.op("vector", vop("scalar_tensor_tensor", out=xsb[:, i, :],
                                   in0=got[:, b, :], scalar=gsv_s[:, i:i + 1],
                                   in1=xsb[:, i, :], op0=ALU.mult, op1=ALU.add),
                     waits=[t_g])
                cb_ev[b] = p.token("vector", "cb")
        p.barrier("moe")

    def sublayer_attn_b():
        stack = contextlib.ExitStack()
        with stack:
            CV = mk_sb(stack, [128, NT, 2, 1040], BF16)
            CQS = mk_sb(stack, [128, NT, 32], BF16)
            bv_stack = contextlib.ExitStack()
            ln_toks = ln_pass(bv_stack)
            lnT, lnT_toks = load_T(bv_stack, ln_dram, ln_toks, T)
            wvc = mk_sb(bv_stack, [128, 8, D], BF16)
            t_wv = p.op("gpsimd", dma(wvc[:], wvr.rearrange("(a p) c -> p a c", p=128)),
                        incs=["wvrl"], dma=True)
            ps_v = mk_psum(bv_stack, [128, 2, 1024], F32)
            ps_q = mk_psum(bv_stack, [128, 2, 32], F32)
            trg = mk_sb(bv_stack, [128, 2, 96], F32)
            cv_toks = []
            v_ev = [None, None]
            CVr = CV[:, :, :, 0:1024].rearrange("p i g (h e) -> p i g h e", e=64)
            for i in range(NT):
                b = i % 2
                w = (lnT_toks + [t_wv, t_ldg] if i < 2 else []) + [v_ev[b]]
                for dc in range(8):
                    p.op("tensor", mm(ps_v[:, b, :], lnT[:, dc, i * 128:(i + 1) * 128],
                                      wvc[:, dc, :], start=(dc == 0), stop=(dc == 7)),
                         waits=w if dc == 0 else [])
                for dc in range(8):
                    p.op("tensor", mm(ps_q[:, b, :], lnT[:, dc, i * 128:(i + 1) * 128],
                                      wqk_b[:, dc, :], start=(dc == 0), stop=(dc == 7)))
                t_mm = p.token("tensor", "bvmm")
                t_th = p.op("scalar", vop("activation", trg[:, b, 0:32], ps_q[:, b, :],
                                          AF.Tanh), waits=[t_mm], incs=["btr"])
                p.op("scalar", vop("activation", trg[:, b, 32:64], trg[:, b, 0:32],
                                   AF.Sin, scale=float(np.pi / 4), bias=scal[:, 1:2]))
                p.op("scalar", vop("activation", trg[:, b, 64:96], trg[:, b, 0:32],
                                   AF.Sin, scale=float(np.pi / 4)))
                t_tr = p.token("scalar", "btr")
                p.op("vector", vop("tensor_copy", out=CQS[:, i, 0:16], in_=trg[:, b, 32:48]),
                     waits=[t_tr])
                p.op("vector", vop("tensor_copy", out=CQS[:, i, 16:32], in_=trg[:, b, 64:80]))
                p.op("vector", vop("tensor_tensor", out=CVr[:, i, 0, :, 0:64],
                                   in0=ps_v.rearrange("p b (h e) -> p b h e", e=64)[:, b],
                                   in1=trg[:, b, 48:64][:, :, None].broadcast_to([128, 16, 64]),
                                   op=ALU.mult), waits=[t_mm])
                p.op("vector", vop("tensor_copy", out=CV[:, i, 0, 1024:1040],
                                   in_=trg[:, b, 48:64]))
                p.op("vector", vop("tensor_tensor", out=CVr[:, i, 1, :, 0:64],
                                   in0=ps_v.rearrange("p b (h e) -> p b h e", e=64)[:, b],
                                   in1=trg[:, b, 80:96][:, :, None].broadcast_to([128, 16, 64]),
                                   op=ALU.mult))
                p.op("vector", vop("tensor_copy", out=CV[:, i, 1, 1024:1040],
                                   in_=trg[:, b, 80:96]))
                t_cv = p.token("vector", "bcv")
                v_ev[b] = t_cv
                cv_toks.append(t_cv)

            bv_stack.close()
            TMh = mk_sb(stack, [16, 2, 1040], BF16)
            TMl = mk_sb(stack, [16, 2, 1040], BF16)
            OFh = mk_sb(stack, [16, 2, 1040], BF16)
            OFl = mk_sb(stack, [16, 2, 1040], BF16)
            of_toks = []
            for g in range(2):
                tg_stack = contextlib.ExitStack()
                ps_t2 = mk_psum(tg_stack, [16, 1040], F32)
                for i in range(NT):
                    p.op("tensor", mm(ps_t2[:, 0:1024], C["stair_b"][:, 15 - i:31 - i],
                                      CV[:, i, g, 0:1024], start=(i == 0),
                                      stop=(i == NT - 1)),
                         waits=[cv_toks[-1]] if i == 0 else [cv_toks[i]])
                for i in range(NT):
                    p.op("tensor", mm(ps_t2[:, 1024:1040], C["stair_b"][:, 15 - i:31 - i],
                                      CV[:, i, g, 1024:1040], start=(i == 0),
                                      stop=(i == NT - 1)))
                t_t2 = p.token("tensor", "btm")
                p.op("vector", vop("tensor_copy", out=TMh[:, g, :], in_=ps_t2[:]),
                     waits=[t_t2])
                p.op("vector", vop("tensor_tensor", out=TMl[:, g, :], in0=ps_t2[:],
                                   in1=TMh[:, g, :], op=ALU.subtract))
                t_tme = p.token("vector", "btme")
                ps_of = mk_psum(tg_stack, [16, 1040], F32)
                p.op("tensor", mm(ps_of[:], C["tri16e_b"][:], TMh[:, g, :], True, False),
                     waits=[t_tme])
                t_of = p.op("tensor", mm(ps_of[:], C["tri16e_b"][:], TMl[:, g, :],
                                         False, True), incs=["bof"])
                p.op("vector", vop("tensor_copy", out=OFh[:, g, :], in_=ps_of[:]),
                     waits=[t_of])
                p.op("vector", vop("tensor_tensor", out=OFl[:, g, :], in0=ps_of[:],
                                   in1=OFh[:, g, :], op=ALU.subtract))
                of_toks.append(p.token("vector", "bofe"))
                tg_stack.close()

            ps_cA = mk_psum(stack, [128, 1040], F32)
            ps_cB = mk_psum(stack, [128, 1040], F32)
            cum = mk_sb(stack, [128, 2, 2, 1040], BF16)
            numb = mk_sb(stack, [128, 2, 1024], BF16)
            tmpb = mk_sb(stack, [128, 2, 1024], BF16)
            den = mk_sb(stack, [128, 2, 2, 16], F32)
            hds = mk_sb(stack, [128, 2, 1024], BF16)
            hd_toks = []
            cum_ev = None
            hd_ev = [None, None]
            for i in range(NT):
                b = i % 2
                w = of_toks + [cum_ev]
                for g, pc in ((0, ps_cA), (1, ps_cB)):
                    for (c0, c1) in ((0, 1024), (1024, 1040)):
                        p.op("tensor", mm(pc[:, c0:c1], C["tri128i_b"][:],
                                          CV[:, i, g, c0:c1], start=True, stop=False),
                             waits=w if (g == 0 and c0 == 0) else [])
                        p.op("tensor", mm(pc[:, c0:c1], C["rs_b"][:, i * 128:(i + 1) * 128],
                                          OFh[:, g, c0:c1], start=False, stop=False))
                        p.op("tensor", mm(pc[:, c0:c1], C["rs_b"][:, i * 128:(i + 1) * 128],
                                          OFl[:, g, c0:c1], start=False, stop=True))
                t_cum = p.token("tensor", "bcum")
                p.op("vector", vop("tensor_copy", out=cum[:, b, 0, :], in_=ps_cA[:]),
                     waits=[t_cum, hd_ev[b]])
                p.op("vector", vop("tensor_copy", out=cum[:, b, 1, :], in_=ps_cB[:]))
                cum_ev = p.token("vector", "bcume")
                p.op("vector", vop("tensor_tensor",
                                   out=numb.rearrange("p b (h e) -> p b h e", e=64)[:, b],
                                   in0=cum[:, b, 0, 0:1024].rearrange("p (h e) -> p h e", e=64),
                                   in1=CQS[:, i, 0:16][:, :, None].broadcast_to([128, 16, 64]),
                                   op=ALU.mult))
                p.op("vector", vop("tensor_tensor",
                                   out=tmpb.rearrange("p b (h e) -> p b h e", e=64)[:, b],
                                   in0=cum[:, b, 1, 0:1024].rearrange("p (h e) -> p h e", e=64),
                                   in1=CQS[:, i, 16:32][:, :, None].broadcast_to([128, 16, 64]),
                                   op=ALU.mult))
                p.op("vector", vop("tensor_tensor", out=numb[:, b, :], in0=numb[:, b, :],
                                   in1=tmpb[:, b, :], op=ALU.add))
                p.op("vector", vop("tensor_tensor", out=den[:, b, 0, :],
                                   in0=cum[:, b, 0, 1024:1040], in1=CQS[:, i, 0:16],
                                   op=ALU.mult))
                p.op("vector", vop("tensor_tensor", out=den[:, b, 1, :],
                                   in0=cum[:, b, 1, 1024:1040], in1=CQS[:, i, 16:32],
                                   op=ALU.mult))
                p.op("vector", vop("tensor_tensor", out=den[:, b, 0, :],
                                   in0=den[:, b, 0, :], in1=den[:, b, 1, :], op=ALU.add))
                p.op("vector", vop("tensor_scalar", out=den[:, b, 0, :],
                                   in0=den[:, b, 0, :], scalar1=scal[:, 2:3], scalar2=None,
                                   op0=ALU.add))
                p.op("vector", vop("reciprocal", den[:, b, 1, :], den[:, b, 0, :]))
                t_dr = p.token("vector", "bden")
                p.op("vector", vop("tensor_tensor",
                                   out=hds.rearrange("p b (h e) -> p b h e", e=64)[:, b],
                                   in0=numb.rearrange("p b (h e) -> p b h e", e=64)[:, b],
                                   in1=den[:, b, 1, :][:, :, None].broadcast_to([128, 16, 64]),
                                   op=ALU.mult), waits=[t_dr])
                t_hd = p.token("vector", "bhds")
                t_hst = p.op("sync", dma(ao_dram.ap()[i * 128:(i + 1) * 128, :],
                                         hds[:, b, :]),
                             waits=[t_hd], incs=[f"bhst{b}"], dma=True)
                hd_ev[b] = t_hst
                hd_toks.append(t_hst)
        stack = contextlib.ExitStack()
        with stack:
            out_proj(stack, wor, None, hd_toks, add_bias=False)
        p.barrier("b3")

    if n_sub == 94:
        sublayer_moe(0)
    elif n_sub == 13:
        sublayer_attn_a()
        sublayer_attn_b()
    else:
        if n_sub >= 1:
            sublayer_attn_a()
        if n_sub >= 2:
            sublayer_moe(0)
        if n_sub >= 3:
            sublayer_attn_b()
        if n_sub >= 4:
            sublayer_moe(1)

    out_toks = []
    for i in range(NT):
        out_toks.append(p.op("gpsimd", dma(x_out[i * 128:(i + 1) * 128, :], xsb[:, i, :]),
                             incs=["xst"], dma=True))
    p.op("gpsimd", vop("memset", scal[:, 7:8], 0.0), waits=out_toks)

    p.emit()
    return nc


def host_inputs(inputs):
    x = np.ascontiguousarray(inputs["L0_attn_a_x"], np.float32)
    wqkv_ = np.ascontiguousarray(inputs["L0_attn_a_W_qkv"], np.float32)
    bqkv_ = np.ascontiguousarray(inputs["L0_attn_a_b_qkv"], np.float32).reshape(1, -1)
    wproj_ = np.ascontiguousarray(inputs["L0_attn_a_W_proj"], np.float32)
    bproj_ = np.ascontiguousarray(inputs["L0_attn_a_b_proj"], np.float32).reshape(1, -1)
    wq = np.transpose(np.asarray(inputs["L0_attn_b_W_Q"], np.float32), (1, 0, 2)).reshape(D, 16)
    wk = np.transpose(np.asarray(inputs["L0_attn_b_W_K"], np.float32), (1, 0, 2)).reshape(D, 16)
    wqk_ = np.ascontiguousarray(np.concatenate([wq, wk], axis=1))
    wvr_ = np.ascontiguousarray(np.transpose(np.asarray(inputs["L0_attn_b_W_V"], np.float32), (1, 0, 2)).reshape(D, D))
    wor_ = np.ascontiguousarray(np.transpose(np.asarray(inputs["L0_attn_b_W_O"], np.float32), (2, 1, 0)).reshape(D, D))
    wg_ = np.ascontiguousarray(inputs["L0_ffn_Wg"], np.float32)
    w1_ = np.ascontiguousarray(inputs["L0_ffn_w1"], np.float32)
    w2_ = np.ascontiguousarray(inputs["L0_ffn_w2"], np.float32)
    in_maps = []
    for c in range(8):
        in_maps.append({
            "x": x[c],
            "wqkv": wqkv_, "bqkv": bqkv_, "wproj": wproj_, "bproj": bproj_,
            "wqk": wqk_, "wvr": wvr_, "wor": wor_, "wg": wg_,
            "w1s": np.ascontiguousarray(w1_[2 * c:2 * c + 2]),
            "w2s": np.ascontiguousarray(w2_[2 * c:2 * c + 2]),
            "coremask": (np.arange(8) < c).astype(np.float32).reshape(8, 1),
            "corec": np.full((128, 1), c * T, np.float32),
        })
    return in_maps


TWO_LAUNCH = True


def kernel(**inputs):
    in_maps = host_inputs(inputs)
    if not TWO_LAUNCH:
        nc = build(4)
        res = run_bass_kernel_spmd(nc, in_maps, core_ids=list(range(8)))
        return np.stack([np.asarray(r["xout"], np.float32) for r in res.results])
    nc = build(3)
    res = run_bass_kernel_spmd(nc, in_maps, core_ids=list(range(8)))
    mid = [np.asarray(r["xout"], np.float32) for r in res.results]
    nc2 = build(94)
    for c in range(8):
        in_maps[c]["x"] = mid[c]
    res2 = run_bass_kernel_spmd(nc2, in_maps, core_ids=list(range(8)))
    return np.stack([np.asarray(r["xout"], np.float32) for r in res2.results])


# version 03
# speedup vs baseline: 1.0046x; 1.0046x over previous
import contextlib
import numpy as np
import ml_dtypes

import concourse.bass as bass
import concourse.mybir as mybir
from concourse.bass import IndirectOffsetOnAxis
from concourse.bass_utils import run_bass_kernel_spmd

BF = ml_dtypes.bfloat16
F32 = mybir.dt.float32
BF16 = mybir.dt.bfloat16
I32 = mybir.dt.int32
U32 = mybir.dt.uint32
AF = mybir.ActivationFunctionType
ALU = mybir.AluOpType
AX = mybir.AxisListType

T = 2048
D = 1024
NT = 16
H = 16
E = 16
CAP = 1024
F_FF = 4096
SHIFT_OFFS = [8, 4, 2, 1, 0]

ENGS = ("sync", "tensor", "vector", "scalar", "gpsimd")


class Prog:

    def __init__(self, nc):
        self.nc = nc
        self.items = {e: [] for e in ENGS}
        self.sems = {}
        self.pending = {e: [] for e in ENGS}
        self._bmark = {e: 0 for e in ENGS}

    def _sem(self, name):
        if name not in self.sems:
            self.sems[name] = [self.nc.alloc_semaphore(name), 0]
        return name

    CHAINED = ("vector", "scalar")

    def op(self, eng, fn, waits=(), incs=(), dma=False, cc=False):
        toks = []
        rec = []
        chain = eng in self.CHAINED and not dma and not cc
        if chain:
            s = self._sem(eng + ":chain")
            prev = self.sems[s][1]
            self.sems[s][1] += 1
            rec.append((s, 1))
            toks.append((s, self.sems[s][1]))
            waits = tuple(waits) + ((s, prev),)
            for _ in incs:
                toks.append((s, self.sems[s][1]))
        else:
            for s0 in incs:
                s = self._sem(eng + ":" + s0)
                if cc:
                    self.sems[s][1] += 1
                    rec.append((s, None))
                else:
                    d = 16 if dma else 1
                    self.sems[s][1] += d
                    rec.append((s, d))
                toks.append((s, self.sems[s][1]))
        w = tuple(self.pending[eng]) + tuple(t for t in waits if t is not None)
        self.pending[eng] = []
        self.items[eng].append([fn, w, tuple(rec), dma or cc])
        if not toks:
            return None
        return toks[0] if len(toks) == 1 else toks[0] if chain else toks

    def token(self, eng, name="tk"):
        if eng in self.CHAINED:
            s = eng + ":chain"
            return (s, self.sems[s][1])
        s = self._sem(eng + ":" + name)
        item = self.items[eng][-1]
        d = 16 if item[3] else 1
        self.sems[s][1] += d
        item[2] = item[2] + ((s, d),)
        return (s, self.sems[s][1])

    def barrier(self, name):
        for e in ENGS:
            if len(self.items[e]) > self._bmark[e]:
                item = self.items[e][-1]
                if not item[2]:
                    s = self._sem("B:" + name + e)
                    d = 16 if item[3] else 1
                    item[2] = item[2] + ((s, d),)
                    self.sems[s][1] += d
        snap = getattr(self, "_bsnap", {})
        toks = [(s, v[1]) for s, v in self.sems.items() if v[1] > snap.get(s, 0)]
        self._bsnap = {s: v[1] for s, v in self.sems.items()}
        for e in ENGS:
            self.pending[e].extend(toks)
            self._bmark[e] = len(self.items[e])
        return toks

    def emit(self):
        with self.nc.Block() as block:
            for eng_name in ENGS:
                items = self.items[eng_name]

                def make_body(items=items):
                    def body(eng):
                        done = {}
                        for fn, waits, incs, _ in items:
                            wmax = {}
                            for (s, v) in waits:
                                if v > wmax.get(s, -1):
                                    wmax[s] = v
                            for s, v in wmax.items():
                                if done.get(s, -1) < v:
                                    eng.wait_ge(self.sems[s][0], v)
                                    done[s] = v
                            inst = fn(eng)
                            for (s, d) in incs:
                                if d is None:
                                    inst.then_inc(self.sems[s][0])
                                else:
                                    inst.then_inc(self.sems[s][0], d)
                    return body

                getattr(block, eng_name)(make_body())


def dma(out, in_, **kw):
    def fn(eng):
        return eng.dma_start(out=out, in_=in_, **kw)
    return fn


def dmat(out, in_):
    def fn(eng):
        return eng.dma_start_transpose(out, in_)
    return fn


def vop(fn_name, *a, **kw):
    def fn(eng):
        return getattr(eng, fn_name)(*a, **kw)
    return fn


def igather(out, in_, idx):
    def fn(eng):
        return eng.indirect_dma_start(
            out=out, out_offset=None, in_=in_,
            in_offset=IndirectOffsetOnAxis(ap=idx, axis=0))
    return fn


def iscatter(out, idx, in_):
    def fn(eng):
        return eng.indirect_dma_start(
            out=out, out_offset=IndirectOffsetOnAxis(ap=idx, axis=0),
            in_=in_, in_offset=None)
    return fn


def collective(kind, op_, ins, outs):
    def fn(eng):
        return eng.collective_compute(
            kind, op_, replica_groups=[list(range(8))], ins=ins, outs=outs)
    return fn


STAGE = [99]
MSTAGE = [99]


def build(n_sub=4):
    nc = bass.Bass("TRN2", target_bir_lowering=False)
    p = Prog(nc)

    x_in = nc.dram_tensor("x", [T, D], F32, kind="ExternalInput").ap()
    x_out = nc.dram_tensor("xout", [T, D], F32, kind="ExternalOutput").ap()
    wqkv = nc.dram_tensor("wqkv", [D, 3 * D], F32, kind="ExternalInput").ap()
    bqkv = nc.dram_tensor("bqkv", [1, 3 * D], F32, kind="ExternalInput").ap()
    wproj = nc.dram_tensor("wproj", [D, D], F32, kind="ExternalInput").ap()
    bproj = nc.dram_tensor("bproj", [1, D], F32, kind="ExternalInput").ap()
    wqk = nc.dram_tensor("wqk", [D, 32], F32, kind="ExternalInput").ap()
    wvr = nc.dram_tensor("wvr", [D, D], F32, kind="ExternalInput").ap()
    wor = nc.dram_tensor("wor", [D, D], F32, kind="ExternalInput").ap()
    wg = nc.dram_tensor("wg", [D, E], F32, kind="ExternalInput").ap()
    w1s = nc.dram_tensor("w1s", [2, D, F_FF], F32, kind="ExternalInput").ap()
    w2s = nc.dram_tensor("w2s", [2, F_FF, D], F32, kind="ExternalInput").ap()
    coremask = nc.dram_tensor("coremask", [8, 1], F32, kind="ExternalInput").ap()
    corec = nc.dram_tensor("corec", [128, 1], F32, kind="ExternalInput").ap()

    ln_dram = nc.dram_tensor("ln_dram", [T, D], BF16)
    ao_dram = nc.dram_tensor("ao_dram", [T, D], BF16)
    lnall_c = [nc.dram_tensor(f"lnall{i}", [8 * T, D], BF16) for i in range(2)]
    counts_c = [nc.dram_tensor(f"counts_d{i}", [1, E], F32) for i in range(2)]
    agcounts_c = [nc.dram_tensor(f"agcounts_d{i}", [8, E], F32) for i in range(2)]
    idtab_c = [nc.dram_tensor(f"idtab{i}", [E * CAP + 128, 1], F32) for i in range(2)]
    idall_c = [nc.dram_tensor(f"idall{i}", [E * CAP + 128, 1], F32) for i in range(2)]
    xe_c = [nc.dram_tensor(f"xe_dram{i}", [2 * CAP, D], BF16) for i in range(2)]
    eo_c = [nc.dram_tensor(f"eo_dram{i}", [2 * CAP, D], BF16) for i in range(2)]
    expall_c = [nc.dram_tensor(f"expall{i}", [E * CAP, D], BF16) for i in range(2)]

    def c_bf(a):
        return np.asarray(a, np.float32).astype(BF)

    stair = np.zeros((128, 31), np.float32)
    stair[:, 15] = 1.0
    rs = np.zeros((16, 16 * 128), np.float32)
    for i in range(16):
        rs[i, i * 128:(i + 1) * 128] = 1.0
    jl, tl = np.meshgrid(np.arange(128), np.arange(128), indexing="ij")
    consts_np = {
        "tri128i_b": c_bf(np.triu(np.ones((128, 128)))),
        "tri128e_f": np.triu(np.ones((128, 128), np.float32), 1),
        "tri128e_b": c_bf(np.triu(np.ones((128, 128)), 1)),
        "tri16e_b": c_bf(np.triu(np.ones((16, 16)), 1)),
        "tri16e_f": np.triu(np.ones((16, 16), np.float32), 1),
        "stair_b": c_bf(stair),
        "stair_f": stair,
        "rs_b": c_bf(rs),
        "rs_f": rs,
        "ones16c_f": np.ones((16, 1), np.float32),
        "ones1x128_b": c_bf(np.ones((1, 128))),
        "ones1x128_f": np.ones((1, 128), np.float32),
        "ones1xT_b": c_bf(np.ones((1, 1024))),
        "onescol_b": c_bf(np.ones((128, 1))),
        "iota16_f": np.broadcast_to(np.arange(16, dtype=np.float32), (128, 16)).copy(),
        "pidx_f": np.arange(128, dtype=np.float32).reshape(128, 1),
        "mask_i": c_bf((jl == tl).astype(np.float32)),
        "mask_prev": c_bf(np.isin(jl - tl, [0, 64, 96, 112, 120, 124, 126, 127]).astype(np.float32)),
        "mask_cur": c_bf(np.isin(tl - jl, [0, 1, 2, 4, 8, 16, 32, 64]).astype(np.float32)),
    }
    const_dram = {k: nc.inline_tensor(v, name="c_" + k) for k, v in consts_np.items()}

    sb = nc.alloc_sbuf_tensor
    xsb = sb("xsb", [128, NT, D], F32)
    C = {}
    for k, v in consts_np.items():
        if k in ("rs_f", "tri128e_f"):
            continue
        C[k] = sb("s_" + k, list(v.shape), BF16 if v.dtype == BF else F32)
    scal = sb("scal", [128, 8], F32)
    bqkv_b = sb("bqkv_b", [1, 3 * D], BF16)
    bproj_b = sb("bproj_b", [1, D], BF16)
    wqk_b = sb("wqk_b", [128, 8, 32], BF16)
    wg_b = sb("wg_b", [128, 8, E], BF16)
    cmask_s = sb("cmask_s", [8, 1], F32)
    corec_s = sb("corec_s", [128, 1], F32)
    idxf_s = sb("idxf_s", [128, NT], F32)
    gsv_s = sb("gsv_s", [128, NT], F32)
    slot2_s = sb("slot2_s", [128, NT], I32)
    oneh_s = sb("oneh_s", [128, NT, E], F32)

    for k in C:
        p.op("sync", dma(C[k][:], const_dram[k].ap()[:, :]), incs=["ldc"], dma=True)
    p.op("sync", dma(cmask_s[:], coremask[:, :]), incs=["ldc"], dma=True)
    p.op("sync", dma(corec_s[:], corec[:, :]), incs=["ldc"], dma=True)
    p.op("gpsimd", dma(bqkv_b[:], bqkv[:, :]), incs=["ldg"], dma=True)
    p.op("gpsimd", dma(bproj_b[:], bproj[:, :]), incs=["ldg"], dma=True)
    p.op("gpsimd", dma(wqk_b[:], wqk.rearrange("(a p) c -> p a c", p=128)),
         incs=["ldg"], dma=True)
    t_ldg = p.op("gpsimd", dma(wg_b[:], wg.rearrange("(a p) c -> p a c", p=128)),
                 incs=["ldg"], dma=True)
    t_ldc = ("sync:ldc", p.sems["sync:ldc"][1])

    p.op("vector", vop("memset", scal[:, 0:1], 1e-5))
    p.op("vector", vop("memset", scal[:, 1:2], float(np.pi / 2)))
    p.op("vector", vop("memset", scal[:, 2:3], 1e-6))
    tk_scal = p.token("vector", "scal")

    tk_x = []
    for i in range(NT):
        tk_x.append(p.op("sync", dma(xsb[:, i, :], x_in[i * 128:(i + 1) * 128, :]),
                         incs=["ldx"], dma=True))

    psc = [0]
    sbc = [0]

    def mk_psum(stack, shape, dt=F32):
        psc[0] += 1
        return stack.enter_context(nc.psum_tensor(f"ps{psc[0]}", shape, dt))

    def mk_sb(stack, shape, dt):
        sbc[0] += 1
        return stack.enter_context(nc.sbuf_tensor(f"sb{sbc[0]}", shape, dt))

    def mm(out, lhsT, rhs, start, stop):
        def fn(eng):
            n = rhs.shape[-1]
            if n <= 512:
                return eng.matmul(out, lhsT, rhs, start=start, stop=stop)
            inst = None
            for c0 in range(0, n, 512):
                c1 = min(c0 + 512, n)
                inst = eng.matmul(out[..., c0:c1], lhsT, rhs[..., c0:c1],
                                  start=start, stop=stop)
            return inst
        return fn

    def ln_pass(stack, extra_wait=()):
        lnb = mk_sb(stack, [128, 2, D], BF16)
        sqs = mk_sb(stack, [128, 1, D], BF16)
        st8 = mk_sb(stack, [128, 2, 8], F32)
        toks = []
        prev = [None, None]
        for i in range(NT):
            b = i % 2
            w = list(extra_wait) + [tk_scal, prev[b]]
            t_sq = p.op("scalar", vop("activation", sqs[:, 0, :], xsb[:, i, :], AF.Square,
                                      accum_out=st8[:, b, 1:2]), waits=w, incs=["lnsq"])
            p.op("vector", vop("reduce_sum", st8[:, b, 0:1], xsb[:, i, :], axis=AX.X),
                 waits=w)
            p.op("vector", vop("tensor_scalar", out=st8[:, b, 2:3], in0=st8[:, b, 0:1],
                               scalar1=1.0 / D, scalar2=None, op0=ALU.mult), waits=[t_sq])
            p.op("vector", vop("tensor_scalar", out=st8[:, b, 3:4], in0=st8[:, b, 2:3],
                               scalar1=-1.0, scalar2=None, op0=ALU.mult))
            p.op("vector", vop("tensor_tensor", out=st8[:, b, 4:5], in0=st8[:, b, 2:3],
                               in1=st8[:, b, 2:3], op=ALU.mult))
            p.op("vector", vop("tensor_scalar", out=st8[:, b, 5:6], in0=st8[:, b, 1:2],
                               scalar1=1.0 / D, scalar2=None, op0=ALU.mult))
            p.op("vector", vop("tensor_tensor", out=st8[:, b, 4:5], in0=st8[:, b, 5:6],
                               in1=st8[:, b, 4:5], op=ALU.subtract))
            t_var = p.token("vector", "lnv")
            t_sd = p.op("scalar", vop("activation", st8[:, b, 6:7], st8[:, b, 4:5],
                                      AF.Sqrt, bias=scal[:, 0:1]), waits=[t_var], incs=["lnsd"])
            p.op("vector", vop("reciprocal", st8[:, b, 7:8], st8[:, b, 6:7]), waits=[t_sd])
            t_r = p.token("vector", "lnv")
            p.op("vector", vop("tensor_scalar", out=lnb[:, b, :], in0=xsb[:, i, :],
                               scalar1=st8[:, b, 3:4], scalar2=st8[:, b, 7:8],
                               op0=ALU.add, op1=ALU.mult), waits=[t_r])
            t_ln = p.token("vector", "lnv")
            tst = p.op("sync", dma(ln_dram.ap()[i * 128:(i + 1) * 128, :], lnb[:, b, :]),
                       waits=[t_ln], incs=[f"lnst{b}"], dma=True)
            prev[b] = tst
            toks.append(tst)
        return toks

    def load_T_into(bufT, src_dram, wait_toks, rows=None):
        toks = []
        for F_ in range(8):
            srcap = src_dram.ap() if rows is None else src_dram.ap()[rows[0]:rows[1], :]
            toks.append(p.op("sync", dmat(bufT[:, F_, :],
                                          srcap[:, F_ * 128:(F_ + 1) * 128]),
                             waits=wait_toks, incs=["ldT"], dma=True))
        return toks

    def load_T(stack, src_dram, wait_toks, width):
        bufT = mk_sb(stack, [128, 8, width], BF16)
        return bufT, load_T_into(bufT, src_dram, wait_toks)

    def out_proj(stack, w_dram, bias_sb, src_toks, add_bias):
        aoT, aoT_toks = load_T(stack, ao_dram, src_toks, T)
        wpr = mk_sb(stack, [128, 8, D], BF16)
        t_w = p.op("gpsimd", dma(wpr[:], w_dram.rearrange("(a p) c -> p a c", p=128)),
                   waits=[src_toks[-1]], incs=["wprld"], dma=True)
        ps_pr = mk_psum(stack, [128, 2, D], F32)
        prev = [None, None]
        for i in range(NT):
            b = i % 2
            w = ([t_w] + aoT_toks if i < 2 else []) + [prev[b]]
            for dc in range(8):
                p.op("tensor", mm(ps_pr[:, b, :], aoT[:, dc, i * 128:(i + 1) * 128],
                                  wpr[:, dc, :], start=(dc == 0),
                                  stop=(dc == 7 and not add_bias)),
                     waits=w if dc == 0 else [])
            if add_bias:
                p.op("tensor", mm(ps_pr[:, b, :], C["ones1x128_b"][:], bias_sb[:, :],
                                  start=False, stop=True))
            t_mm = p.token("tensor", "prmm")
            p.op("vector", vop("tensor_tensor", out=xsb[:, i, :], in0=xsb[:, i, :],
                               in1=ps_pr[:, b, :], op=ALU.add), waits=[t_mm])
            prev[b] = p.token("vector", "pradd")

    def sublayer_attn_a():
        stack = contextlib.ExitStack()
        with stack:
            lnT = mk_sb(stack, [128, 8, T], BF16)
            ln_stack = contextlib.ExitStack()
            ln_toks = ln_pass(ln_stack, extra_wait=[tk_x[-1], t_ldc, t_ldg])
            lnT_toks = load_T_into(lnT, ln_dram, ln_toks)
            ln_stack.close()
            p.barrier("a1ln")
            if STAGE[0] < 2:
                return

            ao_toks = []
            mask_of = {8: "mask_i", 4: "mask_i", 2: "mask_i", 1: "mask_prev", 0: "mask_cur"}
            for hg in range(2):
                hstack = contextlib.ExitStack()
                with hstack:
                    qP = mk_sb(hstack, [128, 4, T], BF16)
                    kz = mk_sb(hstack, [128, 8, T], BF16)
                    wch = mk_sb(hstack, [128, 2, 8, 128], BF16)
                    p.op("vector", vop("memset", kz[:], 0.0))
                    t_kz0 = p.token("vector")
                    qk_stack = contextlib.ExitStack()
                    pss = [mk_psum(qk_stack, [128, 512], F32),
                           mk_psum(qk_stack, [128, 512], F32)]
                    ev = [None, None]
                    for fcl in range(8):
                        wb = fcl % 2
                        col0 = hg * 512 + fcl * 128 if fcl < 4 else 1024 + hg * 512 + (fcl - 4) * 128
                        t_wl = p.op("gpsimd", dma(
                            wch[:, wb, :, :],
                            wqkv.rearrange("(a p) c -> p a c", p=128)[:, :, col0:col0 + 128]),
                            waits=[ev[0], ev[1]] if fcl >= 2 else [], incs=[f"wqkl{wb}"], dma=True)
                        for g in range(4):
                            pb = (fcl * 4 + g) % 2
                            w = [t_wl, ev[pb]] + (lnT_toks if fcl == 0 and g == 0 else [])
                            for dc in range(8):
                                p.op("tensor", mm(pss[pb][:], wch[:, wb, dc, :],
                                                  lnT[:, dc, g * 512:(g + 1) * 512],
                                                  start=(dc == 0), stop=False),
                                     waits=w if dc == 0 else [])
                            p.op("tensor", mm(pss[pb][:], bqkv_b[0:1, col0:col0 + 128],
                                              C["ones1xT_b"][0:1, 0:512], start=False, stop=True))
                            t_mm = p.token("tensor", "qkmm")
                            if fcl < 4:
                                ev[pb] = p.op("scalar", vop("copy",
                                                            out=qP[:, fcl, g * 512:(g + 1) * 512],
                                                            in_=pss[pb][:]),
                                              waits=[t_mm], incs=["qkev"])
                            else:
                                p.op("scalar", vop("copy",
                                                   out=kz[0:64, (fcl - 4) * 2, g * 512:(g + 1) * 512],
                                                   in_=pss[pb][0:64, :]),
                                     waits=[t_mm, t_kz0])
                                ev[pb] = p.op("scalar", vop("copy",
                                                            out=kz[64:128, (fcl - 4) * 2 + 1, g * 512:(g + 1) * 512],
                                                            in_=pss[pb][64:128, :]),
                                              incs=["qkev"])
                    last_qk = p.token("scalar")
                    qk_stack.close()
                    if STAGE[0] < 3:
                        p.barrier(f"dbg{hg}")
                        continue

                    v1h = mk_sb(hstack, [128, 10, 512], BF16)
                    vch = mk_sb(hstack, [128, 8, 512], BF16)
                    em = mk_sb(hstack, [128, 5, 8, 128], BF16)
                    aob = mk_sb(hstack, [128, 2, 512], BF16)
                    zr = mk_sb(hstack, [128, 2, 8], F32)
                    ps_v = mk_psum(hstack, [128, 2, 512], F32)
                    ps_sc = mk_psum(hstack, [128, 2, 512], F32)
                    ps_av = mk_psum(hstack, [128, 2, 512], F32)
                    ps_z = mk_psum(hstack, [128, 2, 8], F32)
                    t_wv = p.op("gpsimd", dma(
                        vch[:],
                        wqkv.rearrange("(a p) c -> p a c", p=128)[:, :, 2048 + hg * 512:2048 + (hg + 1) * 512]),
                        incs=[f"wvl{hg}"], dma=True)
                    v_done = [None] * NT
                    slot_reader = [None] * 10
                    ev_v = [None, None]
                    sc_ev = [None, None]
                    av_ev = [None, None]
                    msk_prev = None
                    for i in range(NT):
                        b = i % 2
                        sl = i % 10
                        w = ([t_wv] + lnT_toks + [last_qk] if i < 2 else []) + \
                            [ev_v[b], slot_reader[sl]]
                        for dc in range(8):
                            p.op("tensor", mm(ps_v[:, b, :],
                                              lnT[:, dc, i * 128:(i + 1) * 128],
                                              vch[:, dc, :], start=(dc == 0), stop=False),
                                 waits=w if dc == 0 else [])
                        p.op("tensor", mm(ps_v[:, b, :], C["ones1x128_b"][:],
                                          bqkv_b[0:1, 2048 + hg * 512:2048 + (hg + 1) * 512],
                                          start=False, stop=True))
                        t_vm = p.token("tensor", "vmm")
                        tve = p.op("scalar", vop("copy", out=v1h[:, sl, :], in_=ps_v[:, b, :]),
                                   waits=[t_vm], incs=["vev"])
                        ev_v[b] = tve
                        v_done[i] = tve
                        if STAGE[0] < 4:
                            slot_reader[sl] = tve
                            continue
                        if STAGE[0] == 42:
                            pass

                        chunks = [(ci, off) for ci, off in enumerate(SHIFT_OFFS) if i - off >= 0]
                        pairs = [(ci, off, h) for (ci, off) in chunks for h in range(8)]
                        em_groups = []
                        for gi in range(0, len(pairs), 4):
                            grp = pairs[gi:gi + 4]
                            pb = (gi // 4) % 2
                            w = [sc_ev[pb], msk_prev]
                            for k_, (ci, off, h) in enumerate(grp):
                                c = i - off
                                p.op("tensor", mm(ps_sc[:, pb, k_ * 128:(k_ + 1) * 128],
                                                  kz[:, h, c * 128:(c + 1) * 128],
                                                  qP[:, h // 2, i * 128:(i + 1) * 128],
                                                  start=True, stop=True),
                                     waits=w if k_ == 0 else [])
                            t_g = p.token("tensor", "scmm")
                            base = grp[0][0] * 8 + grp[0][2]
                            if STAGE[0] == 41:
                                t_e = p.op("scalar", vop(
                                    "copy",
                                    out=em.rearrange("p c h f -> p (c h) f")[:, base:base + len(grp), :],
                                    in_=ps_sc[:, pb, 0:len(grp) * 128]),
                                    waits=[t_g], incs=["scexp"])
                            else:
                                t_e = p.op("scalar", vop(
                                    "activation",
                                    em.rearrange("p c h f -> p (c h) f")[:, base:base + len(grp), :],
                                    ps_sc.rearrange("p b (k f) -> p b k f", f=128)[:, pb, 0:len(grp), :],
                                    AF.Exp, scale=0.125),
                                    waits=[t_g], incs=["scexp"])
                            sc_ev[pb] = t_e
                            em_groups.append((t_e, grp))
                        if STAGE[0] < 50:
                            msk_prev = sc_ev[0]
                            av_ev[b] = em_groups[-1][0]
                            continue
                        mk_toks = []
                        for (t_e, grp) in em_groups:
                            runs = []
                            for (ci, off, h) in grp:
                                if runs and runs[-1][0] == ci:
                                    runs[-1][2] += 1
                                else:
                                    runs.append([ci, h, 1])
                            for (ci, h0, n) in runs:
                                off = SHIFT_OFFS[ci]
                                p.op("vector", vop("tensor_tensor",
                                                   out=em[:, ci, h0:h0 + n, :],
                                                   in0=em[:, ci, h0:h0 + n, :],
                                                   in1=C[mask_of[off]][:, None, :].broadcast_to([128, n, 128]),
                                                   op=ALU.mult), waits=[t_e])
                            mk_toks.append(p.token("vector", "msk"))
                        msk_prev = mk_toks[-1]
                        w_av = [av_ev[b]] + mk_toks + [v_done[i - off] for (ci, off) in chunks]
                        first = True
                        for h in range(8):
                            for (ci, off) in chunks:
                                p.op("tensor", mm(ps_av[:, b, h * 64:(h + 1) * 64],
                                                  em[:, ci, h, :],
                                                  v1h[:, (i - off) % 10, h * 64:(h + 1) * 64],
                                                  start=(ci == chunks[0][0]),
                                                  stop=(ci == chunks[-1][0])),
                                     waits=w_av if first else [])
                                first = False
                            for (ci, off) in chunks:
                                p.op("tensor", mm(ps_z[:, b, h:h + 1], em[:, ci, h, :],
                                                  C["onescol_b"][:],
                                                  start=(ci == chunks[0][0]),
                                                  stop=(ci == chunks[-1][0])))
                        t_av = p.token("tensor", "avmm")
                        for (ci, off) in chunks:
                            slot_reader[(i - off) % 10] = t_av
                        t_zr = p.op("vector", vop("reciprocal", zr[:, b, :], ps_z[:, b, :]),
                                    waits=[t_av], incs=["zrec"])
                        p.op("vector", vop("tensor_tensor",
                                           out=aob.rearrange("p b (h e) -> p b h e", e=64)[:, b],
                                           in0=ps_av.rearrange("p b (h e) -> p b h e", e=64)[:, b],
                                           in1=zr[:, b, :, None].broadcast_to([128, 8, 64]),
                                           op=ALU.mult), waits=[t_zr])
                        t_ao = p.token("vector", "aon")
                        t_aos = p.op("sync", dma(
                            ao_dram.ap()[i * 128:(i + 1) * 128, hg * 512:(hg + 1) * 512],
                            aob[:, b, :]), waits=[t_ao], incs=[f"aost{b}"], dma=True)
                        av_ev[b] = t_aos
                        ao_toks.append(t_aos)
                p.barrier(f"a1h{hg}")
        if STAGE[0] < 6 or not ao_toks:
            return
        stack = contextlib.ExitStack()
        with stack:
            out_proj(stack, wproj, bproj_b, ao_toks, add_bias=True)
        p.barrier("a1")

    def sublayer_moe(call):
        MS = MSTAGE[0] if call == 1 else 99
        lnall = lnall_c[call]
        counts_d = counts_c[call]
        agcounts_d = agcounts_c[call]
        idtab = idtab_c[call]
        idall = idall_c[call]
        xe_dram = xe_c[call]
        eo_dram = eo_c[call]
        expall = expall_c[call]
        stack = contextlib.ExitStack()
        with stack:
            rs_f = mk_sb(stack, [16, 16 * 128], F32)
            tri128e_f = mk_sb(stack, [128, 128], F32)
            p.op("sync", dma(rs_f[:], const_dram["rs_f"].ap()[:, :]), incs=["ldT"], dma=True)
            t_mc = p.op("sync", dma(tri128e_f[:], const_dram["tri128e_f"].ap()[:, :]),
                        incs=["ldT"], dma=True)
            ln_toks = ln_pass(stack)
            t_ag = p.op("gpsimd", collective("AllGather", ALU.bypass,
                                             [ln_dram.ap().opt()], [lnall.ap().opt()]),
                        waits=ln_toks, incs=[f"agd{call}"], cc=True)
            if MS < 22:
                p.barrier(f"mdbg{call}")
                return
            lnT, lnT_toks = load_T(stack, ln_dram, ln_toks, T)
            if MS < 23:
                p.barrier(f"mdbg{call}")
                return

            ps_lg = mk_psum(stack, [128, 2, E], F32)
            gsc = mk_sb(stack, [128, 2, 4], F32)
            eg = mk_sb(stack, [128, 2, E], F32)
            lg_ev = [None, None]
            oneh_toks = []
            for i in range(NT):
                b = i % 2
                w = (lnT_toks + [t_ldg, t_ldc] if i < 2 else []) + [lg_ev[b]]
                for dc in range(8):
                    p.op("tensor", mm(ps_lg[:, b, :], lnT[:, dc, i * 128:(i + 1) * 128],
                                      wg_b[:, dc, :], start=(dc == 0), stop=(dc == 7)),
                         waits=w if dc == 0 else [])
                t_lg = p.token("tensor", "lgmm")
                if MS < 24:
                    lg_ev[b] = t_lg
                    oneh_toks.append(t_lg)
                    continue
                p.op("vector", vop("reduce_max", gsc[:, b, 3:4], ps_lg[:, b, :], axis=AX.X),
                     waits=[t_lg])
                p.op("vector", vop("tensor_tensor", out=eg[:, b, :], in0=ps_lg[:, b, :],
                                   in1=gsc[:, b, 3:4].broadcast_to([128, E]),
                                   op=ALU.is_equal))
                p.op("vector", vop("tensor_scalar", out=eg[:, b, :], in0=eg[:, b, :],
                                   scalar1=-1.0, scalar2=-1e9, op0=ALU.add, op1=ALU.mult))
                p.op("vector", vop("tensor_tensor", out=eg[:, b, :], in0=eg[:, b, :],
                                   in1=C["iota16_f"][:], op=ALU.add))
                p.op("vector", vop("tensor_reduce", out=idxf_s[:, i:i + 1], in_=eg[:, b, :],
                                   axis=AX.X, op=ALU.min))
                p.op("vector", vop("tensor_scalar", out=gsc[:, b, 0:1], in0=gsc[:, b, 3:4],
                                   scalar1=-1.0, scalar2=None, op0=ALU.mult))
                t_nm = p.token("vector", "gv")
                t_ex = p.op("scalar", vop("activation", eg[:, b, :], ps_lg[:, b, :], AF.Exp,
                                          bias=gsc[:, b, 0:1], accum_out=gsc[:, b, 1:2]),
                            waits=[t_nm], incs=["gexp"])
                p.op("vector", vop("reciprocal", gsc[:, b, 2:3], gsc[:, b, 1:2]),
                     waits=[t_ex])
                p.op("vector", vop("tensor_copy", out=gsv_s[:, i:i + 1], in_=gsc[:, b, 2:3]))
                p.op("vector", vop("tensor_tensor", out=oneh_s[:, i, :],
                                   in0=C["iota16_f"][:],
                                   in1=idxf_s[:, i:i + 1].broadcast_to([128, E]),
                                   op=ALU.is_equal))
                t_oh = p.token("vector", "gv")
                lg_ev[b] = t_oh
                oneh_toks.append(t_oh)


            if MS < 25:
                p.barrier(f"mdbg{call}")
                return
            ps_tm = mk_psum(stack, [16, E], F32)
            for i in range(NT):
                p.op("tensor", mm(ps_tm[:], C["stair_f"][:, 15 - i:31 - i], oneh_s[:, i, :],
                                  start=(i == 0), stop=(i == NT - 1)),
                     waits=[oneh_toks[i]])
            t_tm = p.token("tensor", "tmm")

            tmsb = mk_sb(stack, [16, E], F32)
            offm = mk_sb(stack, [16, E], F32)
            crow = mk_sb(stack, [1, 2, E], F32)
            bco = mk_sb(stack, [128, E], F32)
            agc = mk_sb(stack, [8, E], F32)
            t_te = p.op("vector", vop("tensor_copy", out=tmsb[:], in_=ps_tm[:]),
                        waits=[t_tm], incs=["gv"])
            ps_o16 = mk_psum(stack, [16, E], F32)
            t_om = p.op("tensor", mm(ps_o16[:], C["tri16e_f"][:], tmsb[:], True, True),
                        waits=[t_te], incs=["omm"])
            t_of = p.op("vector", vop("tensor_copy", out=offm[:], in_=ps_o16[:]),
                        waits=[t_om], incs=["gv"])
            offh = mk_sb(stack, [16, E], BF16)
            offl = mk_sb(stack, [16, E], BF16)
            p.op("vector", vop("tensor_copy", out=offh[:], in_=offm[:]))
            p.op("vector", vop("tensor_tensor", out=offl[:], in0=offm[:],
                               in1=offh[:], op=ALU.subtract))
            t_ofhl = p.token("vector")
            ps_c = mk_psum(stack, [1, 2, E], F32)
            t_cm = p.op("tensor", mm(ps_c[:, 0, :], C["ones16c_f"][:], tmsb[:], True, True),
                        waits=[t_te], incs=["cmm"])
            t_ce = p.op("vector", vop("tensor_copy", out=crow[:, 0, :], in_=ps_c[:, 0, :]),
                        waits=[t_cm], incs=["gv"])
            t_cst = p.op("sync", dma(counts_d.ap()[:, :], crow[:, 0, :]),
                         waits=[t_ce], incs=["cst"], dma=True)
            t_cag = p.op("gpsimd", collective("AllGather", ALU.bypass,
                                              [counts_d.ap().opt()], [agcounts_d.ap().opt()]),
                         waits=[t_cst], incs=[f"agc{call}"], cc=True)
            t_agl = p.op("sync", dma(agc[:], agcounts_d.ap()[:, :]),
                         waits=[t_cag], incs=["agl"], dma=True)
            t_co = p.op("tensor", mm(ps_c[:, 1, :], cmask_s[:], agc[:], True, True),
                        waits=[t_agl], incs=["comm"])
            t_coe = p.op("vector", vop("tensor_copy", out=crow[:, 1, :], in_=ps_c[:, 1, :]),
                         waits=[t_co], incs=["gv"])
            ps_bc = mk_psum(stack, [128, E], F32)
            t_bc = p.op("tensor", mm(ps_bc[:], C["ones1x128_f"][:], crow[:, 1, :], True, True),
                        waits=[t_coe], incs=["bcmm"])
            t_bce = p.op("vector", vop("tensor_copy", out=bco[:], in_=ps_bc[:]),
                         waits=[t_bc], incs=["gv"])

            if MS < 3:
                p.barrier(f"mdbg{call}")
                return
            ps_rks = [mk_psum(stack, [128, E], F32), mk_psum(stack, [128, E], F32)]
            oneb = mk_sb(stack, [128, 2, E], BF16)
            lsc = mk_sb(stack, [128, 2, E], F32)
            l8 = mk_sb(stack, [128, 2, 8], F32)
            sl32 = mk_sb(stack, [128, 2, 1], I32)
            zt = mk_sb(stack, [128, 128], F32)
            dumpslot = mk_sb(stack, [128, 1], F32)
            p.op("vector", vop("memset", zt[:], 0.0))
            p.op("vector", vop("tensor_scalar", out=dumpslot[:], in0=C["pidx_f"][:],
                               scalar1=float(E * CAP), scalar2=None, op0=ALU.add))
            t_z = p.token("vector", "gv")
            t_zt0 = p.op("sync", dma(idtab.ap()[0:E * CAP].rearrange("(p a) c -> p (a c)", p=128),
                                     zt[:]), waits=[t_z], incs=["ztab"], dma=True)
            t_zt = p.op("sync", dma(idtab.ap()[E * CAP:E * CAP + 128].rearrange("(p a) c -> p (a c)", p=128),
                                    zt[:, 0:1]), waits=[t_z], incs=["ztab"], dma=True)
            rk_ev = [None, None]
            scat_toks = [t_zt]
            for i in range(NT):
                b = i % 2
                t_ob = p.op("vector", vop("tensor_copy", out=oneb[:, b, :],
                                          in_=oneh_s[:, i, :]), waits=[rk_ev[b]])
                w = [t_bce, t_ofhl, t_ob]
                p.op("tensor", mm(ps_rks[b][:], C["tri128e_b"][:], oneb[:, b, :],
                                  True, False), waits=w)
                p.op("tensor", mm(ps_rks[b][:], C["rs_b"][:, i * 128:(i + 1) * 128],
                                  offh[:], False, False))
                t_rk = p.op("tensor", mm(ps_rks[b][:], C["rs_b"][:, i * 128:(i + 1) * 128],
                                         offl[:], False, True), incs=["rkmm"])
                p.op("vector", vop("tensor_tensor", out=lsc[:, b, :], in0=ps_rks[b][:],
                                   in1=bco[:], op=ALU.add), waits=[t_rk])
                p.op("vector", vop("tensor_tensor", out=lsc[:, b, :], in0=lsc[:, b, :],
                                   in1=oneh_s[:, i, :], op=ALU.mult))
                p.op("vector", vop("reduce_sum", l8[:, b, 0:1], lsc[:, b, :], axis=AX.X))
                p.op("vector", vop("tensor_scalar", out=l8[:, b, 1:2], in0=l8[:, b, 0:1],
                                   scalar1=float(CAP), scalar2=None, op0=ALU.is_lt))
                p.op("vector", vop("tensor_scalar", out=l8[:, b, 2:3], in0=l8[:, b, 1:2],
                                   scalar1=-1.0, scalar2=-1.0, op0=ALU.add, op1=ALU.mult))
                p.op("vector", vop("tensor_tensor", out=l8[:, b, 2:3], in0=l8[:, b, 2:3],
                                   in1=dumpslot[:], op=ALU.mult))
                p.op("vector", vop("tensor_scalar", out=l8[:, b, 3:4], in0=idxf_s[:, i:i + 1],
                                   scalar1=float(CAP), scalar2=None, op0=ALU.mult))
                p.op("vector", vop("tensor_tensor", out=l8[:, b, 4:5], in0=l8[:, b, 3:4],
                                   in1=l8[:, b, 0:1], op=ALU.add))
                p.op("vector", vop("tensor_tensor", out=l8[:, b, 4:5], in0=l8[:, b, 4:5],
                                   in1=l8[:, b, 1:2], op=ALU.mult))
                p.op("vector", vop("tensor_tensor", out=l8[:, b, 4:5], in0=l8[:, b, 4:5],
                                   in1=l8[:, b, 2:3], op=ALU.add))
                if MSTAGE[0] == 312:
                    rk_ev[b] = p.token("vector")
                    scat_toks.append(rk_ev[b])
                    continue
                p.op("vector", vop("tensor_copy", out=sl32[:, b, :], in_=l8[:, b, 4:5]))
                p.op("vector", vop("tensor_scalar", out=l8[:, b, 5:6], in0=l8[:, b, 0:1],
                                   scalar1=float(CAP - 1), scalar2=None, op0=ALU.min))
                p.op("vector", vop("tensor_tensor", out=l8[:, b, 5:6], in0=l8[:, b, 5:6],
                                   in1=l8[:, b, 3:4], op=ALU.add))
                p.op("vector", vop("tensor_copy", out=slot2_s[:, i:i + 1], in_=l8[:, b, 5:6]))
                p.op("vector", vop("tensor_tensor", out=gsv_s[:, i:i + 1],
                                   in0=gsv_s[:, i:i + 1], in1=l8[:, b, 1:2], op=ALU.mult))
                p.op("vector", vop("tensor_tensor", out=l8[:, b, 6:7], in0=C["pidx_f"][:],
                                   in1=corec_s[:], op=ALU.add))
                p.op("vector", vop("tensor_scalar", out=l8[:, b, 6:7], in0=l8[:, b, 6:7],
                                   scalar1=float(i * 128), scalar2=None, op0=ALU.add))
                t_v = p.token("vector", "gv")
                if MS >= 32:
                    t_s = p.op("gpsimd", iscatter(idtab.ap()[:, :], sl32[:, b, :],
                                                  l8[:, b, 6:7]),
                               waits=[t_v, t_zt, scat_toks[-1] if len(scat_toks) > 1 else None],
                               incs=["scat"], dma=True)
                else:
                    t_s = t_v
                rk_ev[b] = t_s
                scat_toks.append(t_s)
            if MS < 33:
                p.barrier(f"mdbg{call}")
                return
            t_ar = p.op("gpsimd", collective("AllReduce", ALU.add,
                                             [idtab.ap().opt()], [idall.ap().opt()]),
                        waits=scat_toks, incs=[f"arid{call}"], cc=True)

            if MS < 4:
                p.barrier(f"mdbg{call}")
                return
            gidb = mk_sb(stack, [128, 16], F32)
            gid32 = mk_sb(stack, [128, 16], I32)
            idv = mk_sb(stack, [128, 16], F32)
            idv32 = mk_sb(stack, [128, 16], I32)
            for k_ in range(16):
                p.op("vector", vop("tensor_scalar", out=gidb[:, k_:k_ + 1], in0=corec_s[:],
                                   scalar1=float(k_ * 128), scalar2=None, op0=ALU.add))
                p.op("vector", vop("tensor_tensor", out=gidb[:, k_:k_ + 1],
                                   in0=gidb[:, k_:k_ + 1], in1=C["pidx_f"][:], op=ALU.add))
            p.op("vector", vop("tensor_copy", out=gid32[:], in_=gidb[:]))
            t_gid = p.token("vector", "gv")
            idl_toks = []
            for k_ in range(16):
                idl_toks.append(p.op("gpsimd", igather(idv[:, k_:k_ + 1], idall.ap()[:, :],
                                                       gid32[:, k_:k_ + 1]),
                                     waits=[t_ar, t_gid], incs=["idl"], dma=True))
            t_iv = p.op("vector", vop("tensor_copy", out=idv32[:], in_=idv[:]),
                        waits=[idl_toks[-1]], incs=["gv"])
            gath_toks = []
            xeb = mk_sb(stack, [128, 2, D], BF16)
            xeb_prev = [None, None]
            for k_ in range(16):
                b_ = k_ % 2
                t_gx = p.op("gpsimd", igather(xeb[:, b_, :], lnall.ap()[:, :],
                                              idv32[:, k_:k_ + 1]),
                            waits=[t_iv, t_ag, xeb_prev[b_]], incs=[f"xeg{b_}"], dma=True)
                t_sx = p.op("sync", dma(xe_dram.ap()[k_ * 128:(k_ + 1) * 128, :],
                                        xeb[:, b_, :]),
                            waits=[t_gx], incs=[f"xest{b_}"], dma=True)
                xeb_prev[b_] = t_sx
                gath_toks.append(t_sx)

        if MS < 5:
            p.barrier(f"mdbg{call}")
            return
        p.barrier(f"moepre{call}")
        stack = contextlib.ExitStack()
        with stack:
            if MSTAGE[0] in (71, 72, 73):
                safei = mk_sb(stack, [128, 1], I32)
                dummy = mk_sb(stack, [128, D], BF16)
                p.op("vector", vop("tensor_copy", out=safei[:], in_=C["pidx_f"][:]))
                t_si = p.token("vector")
            if MSTAGE[0] == 71:
                p.op("gpsimd", igather(dummy[:], lnall.ap()[:, :], safei[:]),
                     waits=[t_si], incs=["cbg0"], dma=True)
                p.barrier(f"mdbg{call}")
                return
            xeT = mk_sb(stack, [128, 8, CAP], BF16)
            xeT_toks0 = []
            if MSTAGE[0] == 72:
                for F_ in range(8):
                    xeT_toks0.append(p.op("sync", dmat(
                        xeT[:, F_, :],
                        xe_dram.ap()[0:CAP, F_ * 128:(F_ + 1) * 128]),
                        waits=gath_toks, incs=["xeT"], dma=True))
            if MSTAGE[0] == 72:
                p.op("gpsimd", igather(dummy[:], lnall.ap()[:, :], safei[:]),
                     waits=[t_si] + xeT_toks0, incs=["cbg0"], dma=True)
                p.barrier(f"mdbg{call}")
                return
            if MSTAGE[0] in (73, 74, 75, 76):
                safei73 = mk_sb(stack, [128, 1], I32)
                dummy73 = mk_sb(stack, [128, D], BF16)
                p.op("vector", vop("tensor_copy", out=safei73[:], in_=C["pidx_f"][:]))
                t_si73 = p.token("vector")
            hT = mk_sb(stack, [128, 32, 512], BF16)
            w1c = mk_sb(stack, [128, 2, 8, 512], BF16)
            w2c = mk_sb(stack, [128, 2, 4, 1024], BF16)
            eob = mk_sb(stack, [128, 2, 1024], BF16)
            eo_toks = []
            o_ev_g = None
            t_lastmm1 = None
            prev_relu = None
            w2_mm_tok = {}
            prev_ost = [None, None]
            for e_ in range(2):
                xeT_toks = []
                for F_ in range(8):
                    xeT_toks.append(p.op("sync", dmat(
                        xeT[:, F_, :],
                        xe_dram.ap()[e_ * CAP:(e_ + 1) * CAP, F_ * 128:(F_ + 1) * 128]),
                        waits=gath_toks + ([t_lastmm1] if e_ else []),
                        incs=["xeT"], dma=True))
                for th in range(2):
                    ph_stack = contextlib.ExitStack()
                    ps_h = mk_psum(ph_stack, [128, 2, 512], F32)
                    h_ev = [None, None]
                    w1_t = [None, None]
                    relu_done = []
                    for slab in range(8):
                        sb_ = slab % 2
                        w1_t[sb_] = p.op("gpsimd", dma(
                            w1c[:, sb_, :, :],
                            w1s[e_].rearrange("(a p) c -> p a c", p=128)[:, :, slab * 512:(slab + 1) * 512]),
                            waits=[relu_done[-5] if len(relu_done) >= 5 else t_lastmm1],
                            incs=[f"w1l{sb_}"], dma=True)
                        for fq in range(4):
                            fc = slab * 4 + fq
                            b = fc % 2
                            w = [w1_t[sb_], h_ev[b]] + \
                                (xeT_toks + [prev_relu, o_ev_g] if fc == 0 else [])
                            for dc in range(8):
                                p.op("tensor", mm(ps_h[:, b, :],
                                                  w1c[:, sb_, dc, fq * 128:(fq + 1) * 128],
                                                  xeT[:, dc, th * 512:(th + 1) * 512],
                                                  start=(dc == 0), stop=(dc == 7)),
                                     waits=w if dc == 0 else [])
                            t_h = p.token("tensor", "hmm")
                            t_lastmm1 = t_h
                            t_r = p.op("scalar", vop("activation", hT[:, fc, :],
                                                     ps_h[:, b, :], AF.Relu),
                                       waits=[t_h], incs=["relu"])
                            h_ev[b] = t_r
                            relu_done.append(t_r)
                    prev_relu = relu_done[-1]
                    ph_stack.close()
                    if MSTAGE[0] == 73 and e_ == 0 and th == 0:
                        p.op("gpsimd", igather(dummy73[:], lnall.ap()[:, :], safei73[:]),
                             waits=[t_si73, relu_done[-1]], incs=["cbg0"], dma=True)
                        p.barrier(f"mdbg{call}")
                        return
                    po_stack = contextlib.ExitStack()
                    ps_o = [mk_psum(po_stack, [128, 1024], F32) for _ in range(4)]
                    w2_t = [None, None]
                    ost_toks = []
                    for fc in range(32):
                        if fc % 4 == 0:
                            slab = fc // 4
                            wkey = (slab - 2) % 2 if slab >= 2 else None
                            w2_t[slab % 2] = p.op("gpsimd", dma(
                                w2c[:, slab % 2, :, :],
                                w2s[e_].rearrange("(a p) c -> p a c", p=128)[:, slab * 4:(slab + 1) * 4, :]),
                                waits=[w2_mm_tok.get(slab - 2) if slab >= 2 else o_ev_g],
                                incs=[f"w2l{slab % 2}"], dma=True)
                        for tq in range(4):
                            w = []
                            if fc == 0:
                                w = [relu_done[-1], o_ev_g]
                            if fc % 4 == 0 and tq == 0:
                                w = w + [w2_t[(fc // 4) % 2]]
                            p.op("tensor", mm(ps_o[tq][:],
                                              hT[:, fc, tq * 128:(tq + 1) * 128],
                                              w2c[:, (fc // 4) % 2, fc % 4, :],
                                              start=(fc == 0), stop=(fc == 31)),
                                 waits=w)
                        if fc % 4 == 3 and fc != 31:
                            w2_mm_tok[fc // 4] = p.token("tensor", "w2use")
                    t_o = p.token("tensor", "omm2")
                    for tq in range(4):
                        b = tq % 2
                        t_oe = p.op("vector", vop("tensor_copy", out=eob[:, b, :],
                                                  in_=ps_o[tq][:]),
                                    waits=[t_o, prev_ost[b]],
                                    incs=["oev"])
                        row0 = (e_ * 8 + th * 4 + tq) * 128
                        t_os = p.op("sync", dma(eo_dram.ap()[row0:row0 + 128, :],
                                                eob[:, b, :]),
                                    waits=[t_oe], incs=[f"ost{b}"], dma=True)
                        ost_toks.append(t_os)
                        prev_ost[b] = t_os
                        eo_toks.append(t_os)
                    o_ev_g = ost_toks[-1]
                    po_stack.close()
                    if MSTAGE[0] == 74 and e_ == 0 and th == 0:
                        p.op("gpsimd", igather(dummy73[:], lnall.ap()[:, :], safei73[:]),
                             waits=[t_si73, o_ev_g], incs=["cbg0"], dma=True)
                        p.barrier(f"mdbg{call}")
                        return
            if MSTAGE[0] == 75:
                p.op("gpsimd", igather(dummy73[:], lnall.ap()[:, :], safei73[:]),
                     waits=[t_si73] + eo_toks, incs=["cbg0"], dma=True)
                p.barrier(f"mdbg{call}")
                return
            t_eag = p.op("gpsimd", collective("AllGather", ALU.bypass,
                                              [eo_dram.ap().opt()], [expall.ap().opt()]),
                         waits=eo_toks, incs=[f"eag{call}"], cc=True)
            if MSTAGE[0] == 76:
                p.op("gpsimd", igather(dummy73[:], lnall.ap()[:, :], safei73[:]),
                     waits=[t_si73, t_eag], incs=["cbg0"], dma=True)
                p.barrier(f"mdbg{call}")
                return

            if MS < 6:
                p.barrier(f"mdbg{call}")
                return
            got = mk_sb(stack, [128, 2, D], BF16)
            cb_ev = [None, None]
            if MSTAGE[0] in (65, 66, 67):
                safei = mk_sb(stack, [128, 1], I32)
                p.op("vector", vop("tensor_copy", out=safei[:], in_=C["pidx_f"][:]))
                t_si = p.token("vector")
                srct = {65: expall, 66: lnall, 67: eo_dram}[MSTAGE[0]]
                t_g = p.op("gpsimd", igather(got[:, 0, :], srct.ap()[:, :], safei[:]),
                           waits=[t_eag, t_si], incs=["cbg0"], dma=True)
                p.barrier(f"mdbg{call}")
                return
            nloop = 1 if MSTAGE[0] == 62 else NT
            srctab = lnall if MSTAGE[0] == 63 else expall
            for i in range(nloop):
                b = i % 2
                gw = [cb_ev[b]] if MSTAGE[0] == 64 else [t_eag, cb_ev[b]]
                t_g = p.op("gpsimd", igather(got[:, b, :], srctab.ap()[:, :],
                                             slot2_s[:, i:i + 1]),
                           waits=gw, incs=[f"cbg{b}"], dma=True)
                if MSTAGE[0] == 61:
                    cb_ev[b] = t_g
                    continue
                p.op("vector", vop("scalar_tensor_tensor", out=xsb[:, i, :],
                                   in0=got[:, b, :], scalar=gsv_s[:, i:i + 1],
                                   in1=xsb[:, i, :], op0=ALU.mult, op1=ALU.add),
                     waits=[t_g])
                cb_ev[b] = p.token("vector", "cb")
        p.barrier("moe")

    def sublayer_attn_b():
        stack = contextlib.ExitStack()
        with stack:
            CV = mk_sb(stack, [128, NT, 2, 1040], BF16)
            CQS = mk_sb(stack, [128, NT, 32], BF16)
            bv_stack = contextlib.ExitStack()
            ln_toks = ln_pass(bv_stack)
            lnT, lnT_toks = load_T(bv_stack, ln_dram, ln_toks, T)
            wvc = mk_sb(bv_stack, [128, 8, D], BF16)
            t_wv = p.op("gpsimd", dma(wvc[:], wvr.rearrange("(a p) c -> p a c", p=128)),
                        incs=["wvrl"], dma=True)
            ps_v = mk_psum(bv_stack, [128, 2, 1024], F32)
            ps_q = mk_psum(bv_stack, [128, 2, 32], F32)
            trg = mk_sb(bv_stack, [128, 2, 96], F32)
            cv_toks = []
            v_ev = [None, None]
            CVr = CV[:, :, :, 0:1024].rearrange("p i g (h e) -> p i g h e", e=64)
            for i in range(NT):
                b = i % 2
                w = (lnT_toks + [t_wv, t_ldg] if i < 2 else []) + [v_ev[b]]
                for dc in range(8):
                    p.op("tensor", mm(ps_v[:, b, :], lnT[:, dc, i * 128:(i + 1) * 128],
                                      wvc[:, dc, :], start=(dc == 0), stop=(dc == 7)),
                         waits=w if dc == 0 else [])
                for dc in range(8):
                    p.op("tensor", mm(ps_q[:, b, :], lnT[:, dc, i * 128:(i + 1) * 128],
                                      wqk_b[:, dc, :], start=(dc == 0), stop=(dc == 7)))
                t_mm = p.token("tensor", "bvmm")
                t_th = p.op("scalar", vop("activation", trg[:, b, 0:32], ps_q[:, b, :],
                                          AF.Tanh), waits=[t_mm], incs=["btr"])
                p.op("scalar", vop("activation", trg[:, b, 32:64], trg[:, b, 0:32],
                                   AF.Sin, scale=float(np.pi / 4), bias=scal[:, 1:2]))
                p.op("scalar", vop("activation", trg[:, b, 64:96], trg[:, b, 0:32],
                                   AF.Sin, scale=float(np.pi / 4)))
                t_tr = p.token("scalar", "btr")
                p.op("vector", vop("tensor_copy", out=CQS[:, i, 0:16], in_=trg[:, b, 32:48]),
                     waits=[t_tr])
                p.op("vector", vop("tensor_copy", out=CQS[:, i, 16:32], in_=trg[:, b, 64:80]))
                p.op("vector", vop("tensor_tensor", out=CVr[:, i, 0, :, 0:64],
                                   in0=ps_v.rearrange("p b (h e) -> p b h e", e=64)[:, b],
                                   in1=trg[:, b, 48:64][:, :, None].broadcast_to([128, 16, 64]),
                                   op=ALU.mult), waits=[t_mm])
                p.op("vector", vop("tensor_copy", out=CV[:, i, 0, 1024:1040],
                                   in_=trg[:, b, 48:64]))
                p.op("vector", vop("tensor_tensor", out=CVr[:, i, 1, :, 0:64],
                                   in0=ps_v.rearrange("p b (h e) -> p b h e", e=64)[:, b],
                                   in1=trg[:, b, 80:96][:, :, None].broadcast_to([128, 16, 64]),
                                   op=ALU.mult))
                p.op("vector", vop("tensor_copy", out=CV[:, i, 1, 1024:1040],
                                   in_=trg[:, b, 80:96]))
                t_cv = p.token("vector", "bcv")
                v_ev[b] = t_cv
                cv_toks.append(t_cv)

            bv_stack.close()
            TMh = mk_sb(stack, [16, 2, 1040], BF16)
            TMl = mk_sb(stack, [16, 2, 1040], BF16)
            OFh = mk_sb(stack, [16, 2, 1040], BF16)
            OFl = mk_sb(stack, [16, 2, 1040], BF16)
            of_toks = []
            for g in range(2):
                tg_stack = contextlib.ExitStack()
                ps_t2 = mk_psum(tg_stack, [16, 1040], F32)
                for i in range(NT):
                    p.op("tensor", mm(ps_t2[:, 0:1024], C["stair_b"][:, 15 - i:31 - i],
                                      CV[:, i, g, 0:1024], start=(i == 0),
                                      stop=(i == NT - 1)),
                         waits=[cv_toks[-1]] if i == 0 else [cv_toks[i]])
                for i in range(NT):
                    p.op("tensor", mm(ps_t2[:, 1024:1040], C["stair_b"][:, 15 - i:31 - i],
                                      CV[:, i, g, 1024:1040], start=(i == 0),
                                      stop=(i == NT - 1)))
                t_t2 = p.token("tensor", "btm")
                p.op("vector", vop("tensor_copy", out=TMh[:, g, :], in_=ps_t2[:]),
                     waits=[t_t2])
                p.op("vector", vop("tensor_tensor", out=TMl[:, g, :], in0=ps_t2[:],
                                   in1=TMh[:, g, :], op=ALU.subtract))
                t_tme = p.token("vector", "btme")
                ps_of = mk_psum(tg_stack, [16, 1040], F32)
                p.op("tensor", mm(ps_of[:], C["tri16e_b"][:], TMh[:, g, :], True, False),
                     waits=[t_tme])
                t_of = p.op("tensor", mm(ps_of[:], C["tri16e_b"][:], TMl[:, g, :],
                                         False, True), incs=["bof"])
                p.op("vector", vop("tensor_copy", out=OFh[:, g, :], in_=ps_of[:]),
                     waits=[t_of])
                p.op("vector", vop("tensor_tensor", out=OFl[:, g, :], in0=ps_of[:],
                                   in1=OFh[:, g, :], op=ALU.subtract))
                of_toks.append(p.token("vector", "bofe"))
                tg_stack.close()

            ps_cA = mk_psum(stack, [128, 1040], F32)
            ps_cB = mk_psum(stack, [128, 1040], F32)
            cum = mk_sb(stack, [128, 2, 2, 1040], BF16)
            numb = mk_sb(stack, [128, 2, 1024], BF16)
            tmpb = mk_sb(stack, [128, 2, 1024], BF16)
            den = mk_sb(stack, [128, 2, 2, 16], F32)
            hds = mk_sb(stack, [128, 2, 1024], BF16)
            hd_toks = []
            cum_ev = None
            hd_ev = [None, None]
            for i in range(NT):
                b = i % 2
                w = of_toks + [cum_ev]
                for g, pc in ((0, ps_cA), (1, ps_cB)):
                    for (c0, c1) in ((0, 1024), (1024, 1040)):
                        p.op("tensor", mm(pc[:, c0:c1], C["tri128i_b"][:],
                                          CV[:, i, g, c0:c1], start=True, stop=False),
                             waits=w if (g == 0 and c0 == 0) else [])
                        p.op("tensor", mm(pc[:, c0:c1], C["rs_b"][:, i * 128:(i + 1) * 128],
                                          OFh[:, g, c0:c1], start=False, stop=False))
                        p.op("tensor", mm(pc[:, c0:c1], C["rs_b"][:, i * 128:(i + 1) * 128],
                                          OFl[:, g, c0:c1], start=False, stop=True))
                t_cum = p.token("tensor", "bcum")
                p.op("vector", vop("tensor_copy", out=cum[:, b, 0, :], in_=ps_cA[:]),
                     waits=[t_cum, hd_ev[b]])
                p.op("vector", vop("tensor_copy", out=cum[:, b, 1, :], in_=ps_cB[:]))
                cum_ev = p.token("vector", "bcume")
                p.op("vector", vop("tensor_tensor",
                                   out=numb.rearrange("p b (h e) -> p b h e", e=64)[:, b],
                                   in0=cum[:, b, 0, 0:1024].rearrange("p (h e) -> p h e", e=64),
                                   in1=CQS[:, i, 0:16][:, :, None].broadcast_to([128, 16, 64]),
                                   op=ALU.mult))
                p.op("vector", vop("tensor_tensor",
                                   out=tmpb.rearrange("p b (h e) -> p b h e", e=64)[:, b],
                                   in0=cum[:, b, 1, 0:1024].rearrange("p (h e) -> p h e", e=64),
                                   in1=CQS[:, i, 16:32][:, :, None].broadcast_to([128, 16, 64]),
                                   op=ALU.mult))
                p.op("vector", vop("tensor_tensor", out=numb[:, b, :], in0=numb[:, b, :],
                                   in1=tmpb[:, b, :], op=ALU.add))
                p.op("vector", vop("tensor_tensor", out=den[:, b, 0, :],
                                   in0=cum[:, b, 0, 1024:1040], in1=CQS[:, i, 0:16],
                                   op=ALU.mult))
                p.op("vector", vop("tensor_tensor", out=den[:, b, 1, :],
                                   in0=cum[:, b, 1, 1024:1040], in1=CQS[:, i, 16:32],
                                   op=ALU.mult))
                p.op("vector", vop("tensor_tensor", out=den[:, b, 0, :],
                                   in0=den[:, b, 0, :], in1=den[:, b, 1, :], op=ALU.add))
                p.op("vector", vop("tensor_scalar", out=den[:, b, 0, :],
                                   in0=den[:, b, 0, :], scalar1=scal[:, 2:3], scalar2=None,
                                   op0=ALU.add))
                p.op("vector", vop("reciprocal", den[:, b, 1, :], den[:, b, 0, :]))
                t_dr = p.token("vector", "bden")
                p.op("vector", vop("tensor_tensor",
                                   out=hds.rearrange("p b (h e) -> p b h e", e=64)[:, b],
                                   in0=numb.rearrange("p b (h e) -> p b h e", e=64)[:, b],
                                   in1=den[:, b, 1, :][:, :, None].broadcast_to([128, 16, 64]),
                                   op=ALU.mult), waits=[t_dr])
                t_hd = p.token("vector", "bhds")
                t_hst = p.op("sync", dma(ao_dram.ap()[i * 128:(i + 1) * 128, :],
                                         hds[:, b, :]),
                             waits=[t_hd], incs=[f"bhst{b}"], dma=True)
                hd_ev[b] = t_hst
                hd_toks.append(t_hst)
        stack = contextlib.ExitStack()
        with stack:
            out_proj(stack, wor, None, hd_toks, add_bias=False)
        p.barrier("b3")

    if n_sub == 94:
        sublayer_moe(0)
    elif n_sub == 95:
        sublayer_moe(0)
        sublayer_moe(1)
    elif n_sub == 13:
        sublayer_attn_a()
        sublayer_attn_b()
    else:
        if n_sub >= 1:
            sublayer_attn_a()
        if n_sub >= 2:
            sublayer_moe(0)
        if n_sub >= 3:
            sublayer_attn_b()
        if n_sub >= 4:
            sublayer_moe(1)

    out_toks = []
    for i in range(NT):
        out_toks.append(p.op("gpsimd", dma(x_out[i * 128:(i + 1) * 128, :], xsb[:, i, :]),
                             incs=["xst"], dma=True))
    p.op("gpsimd", vop("memset", scal[:, 7:8], 0.0), waits=out_toks)

    p.emit()
    return nc


def host_inputs(inputs):
    x = np.ascontiguousarray(inputs["L0_attn_a_x"], np.float32)
    wqkv_ = np.ascontiguousarray(inputs["L0_attn_a_W_qkv"], np.float32)
    bqkv_ = np.ascontiguousarray(inputs["L0_attn_a_b_qkv"], np.float32).reshape(1, -1)
    wproj_ = np.ascontiguousarray(inputs["L0_attn_a_W_proj"], np.float32)
    bproj_ = np.ascontiguousarray(inputs["L0_attn_a_b_proj"], np.float32).reshape(1, -1)
    wq = np.transpose(np.asarray(inputs["L0_attn_b_W_Q"], np.float32), (1, 0, 2)).reshape(D, 16)
    wk = np.transpose(np.asarray(inputs["L0_attn_b_W_K"], np.float32), (1, 0, 2)).reshape(D, 16)
    wqk_ = np.ascontiguousarray(np.concatenate([wq, wk], axis=1))
    wvr_ = np.ascontiguousarray(np.transpose(np.asarray(inputs["L0_attn_b_W_V"], np.float32), (1, 0, 2)).reshape(D, D))
    wor_ = np.ascontiguousarray(np.transpose(np.asarray(inputs["L0_attn_b_W_O"], np.float32), (2, 1, 0)).reshape(D, D))
    wg_ = np.ascontiguousarray(inputs["L0_ffn_Wg"], np.float32)
    w1_ = np.ascontiguousarray(inputs["L0_ffn_w1"], np.float32)
    w2_ = np.ascontiguousarray(inputs["L0_ffn_w2"], np.float32)
    in_maps = []
    for c in range(8):
        in_maps.append({
            "x": x[c],
            "wqkv": wqkv_, "bqkv": bqkv_, "wproj": wproj_, "bproj": bproj_,
            "wqk": wqk_, "wvr": wvr_, "wor": wor_, "wg": wg_,
            "w1s": np.ascontiguousarray(w1_[2 * c:2 * c + 2]),
            "w2s": np.ascontiguousarray(w2_[2 * c:2 * c + 2]),
            "coremask": (np.arange(8) < c).astype(np.float32).reshape(8, 1),
            "corec": np.full((128, 1), c * T, np.float32),
        })
    return in_maps


TWO_LAUNCH = False


def kernel(**inputs):
    in_maps = host_inputs(inputs)
    if not TWO_LAUNCH:
        nc = build(4)
        res = run_bass_kernel_spmd(nc, in_maps, core_ids=list(range(8)))
        return np.stack([np.asarray(r["xout"], np.float32) for r in res.results])
    nc = build(3)
    res = run_bass_kernel_spmd(nc, in_maps, core_ids=list(range(8)))
    mid = [np.asarray(r["xout"], np.float32) for r in res.results]
    nc2 = build(94)
    for c in range(8):
        in_maps[c]["x"] = mid[c]
    res2 = run_bass_kernel_spmd(nc2, in_maps, core_ids=list(range(8)))
    return np.stack([np.asarray(r["xout"], np.float32) for r in res2.results])


# version 04 (final)
# speedup vs baseline: 1.0680x; 1.0631x over previous
import contextlib
import numpy as np
import ml_dtypes

import concourse.bass as bass
import concourse.mybir as mybir
from concourse.bass import IndirectOffsetOnAxis
from concourse.bass_utils import run_bass_kernel_spmd

BF = ml_dtypes.bfloat16
F32 = mybir.dt.float32
BF16 = mybir.dt.bfloat16
I32 = mybir.dt.int32
U32 = mybir.dt.uint32
AF = mybir.ActivationFunctionType
ALU = mybir.AluOpType
AX = mybir.AxisListType

T = 2048
D = 1024
NT = 16
H = 16
E = 16
CAP = 1024
F_FF = 4096
SHIFT_OFFS = [8, 4, 2, 1, 0]

ENGS = ("sync", "tensor", "vector", "scalar", "gpsimd")


class Prog:

    def __init__(self, nc):
        self.nc = nc
        self.items = {e: [] for e in ENGS}
        self.sems = {}
        self.pending = {e: [] for e in ENGS}
        self._bmark = {e: 0 for e in ENGS}

    def _sem(self, name):
        if name not in self.sems:
            self.sems[name] = [self.nc.alloc_semaphore(name), 0]
        return name

    CHAINED = ("vector", "scalar")

    def op(self, eng, fn, waits=(), incs=(), dma=False, cc=False):
        toks = []
        rec = []
        chain = eng in self.CHAINED and not dma and not cc
        if chain:
            s = self._sem(eng + ":chain")
            prev = self.sems[s][1]
            self.sems[s][1] += 1
            rec.append((s, 1))
            toks.append((s, self.sems[s][1]))
            waits = tuple(waits) + ((s, prev),)
            for _ in incs:
                toks.append((s, self.sems[s][1]))
        else:
            for s0 in incs:
                s = self._sem(eng + ":" + s0)
                if cc:
                    self.sems[s][1] += 1
                    rec.append((s, None))
                else:
                    d = 16 if dma else 1
                    self.sems[s][1] += d
                    rec.append((s, d))
                toks.append((s, self.sems[s][1]))
        w = tuple(self.pending[eng]) + tuple(t for t in waits if t is not None)
        self.pending[eng] = []
        self.items[eng].append([fn, w, tuple(rec), dma or cc])
        if not toks:
            return None
        return toks[0] if len(toks) == 1 else toks[0] if chain else toks

    def token(self, eng, name="tk"):
        if eng in self.CHAINED:
            s = eng + ":chain"
            return (s, self.sems[s][1])
        s = self._sem(eng + ":" + name)
        item = self.items[eng][-1]
        d = 16 if item[3] else 1
        self.sems[s][1] += d
        item[2] = item[2] + ((s, d),)
        return (s, self.sems[s][1])

    def barrier(self, name):
        for e in ENGS:
            if len(self.items[e]) > self._bmark[e]:
                item = self.items[e][-1]
                if not item[2]:
                    s = self._sem("B:" + name + e)
                    d = 16 if item[3] else 1
                    item[2] = item[2] + ((s, d),)
                    self.sems[s][1] += d
        snap = getattr(self, "_bsnap", {})
        toks = [(s, v[1]) for s, v in self.sems.items() if v[1] > snap.get(s, 0)]
        self._bsnap = {s: v[1] for s, v in self.sems.items()}
        for e in ENGS:
            self.pending[e].extend(toks)
            self._bmark[e] = len(self.items[e])
        return toks

    def emit(self):
        with self.nc.Block() as block:
            for eng_name in ENGS:
                items = self.items[eng_name]

                def make_body(items=items):
                    def body(eng):
                        done = {}
                        for fn, waits, incs, _ in items:
                            wmax = {}
                            for (s, v) in waits:
                                if v > wmax.get(s, -1):
                                    wmax[s] = v
                            for s, v in wmax.items():
                                if done.get(s, -1) < v:
                                    eng.wait_ge(self.sems[s][0], v)
                                    done[s] = v
                            inst = fn(eng)
                            for (s, d) in incs:
                                if d is None:
                                    inst.then_inc(self.sems[s][0])
                                else:
                                    inst.then_inc(self.sems[s][0], d)
                    return body

                getattr(block, eng_name)(make_body())


def dma(out, in_, **kw):
    def fn(eng):
        return eng.dma_start(out=out, in_=in_, **kw)
    return fn


def dmat(out, in_):
    def fn(eng):
        return eng.dma_start_transpose(out, in_)
    return fn


def vop(fn_name, *a, **kw):
    def fn(eng):
        return getattr(eng, fn_name)(*a, **kw)
    return fn


def igather(out, in_, idx):
    def fn(eng):
        return eng.indirect_dma_start(
            out=out, out_offset=None, in_=in_,
            in_offset=IndirectOffsetOnAxis(ap=idx, axis=0))
    return fn


def iscatter(out, idx, in_):
    def fn(eng):
        return eng.indirect_dma_start(
            out=out, out_offset=IndirectOffsetOnAxis(ap=idx, axis=0),
            in_=in_, in_offset=None)
    return fn


def collective(kind, op_, ins, outs):
    def fn(eng):
        return eng.collective_compute(
            kind, op_, replica_groups=[list(range(8))], ins=ins, outs=outs)
    return fn


STAGE = [99]
MSTAGE = [99]


def build(n_sub=4):
    nc = bass.Bass("TRN2", target_bir_lowering=False)
    p = Prog(nc)

    x_in = nc.dram_tensor("x", [T, D], F32, kind="ExternalInput").ap()
    x_out = nc.dram_tensor("xout", [T, D], F32, kind="ExternalOutput").ap()
    wqkv = nc.dram_tensor("wqkv", [D, 3 * D], F32, kind="ExternalInput").ap()
    bqkv = nc.dram_tensor("bqkv", [1, 3 * D], F32, kind="ExternalInput").ap()
    wproj = nc.dram_tensor("wproj", [D, D], F32, kind="ExternalInput").ap()
    bproj = nc.dram_tensor("bproj", [1, D], F32, kind="ExternalInput").ap()
    wqk = nc.dram_tensor("wqk", [D, 32], F32, kind="ExternalInput").ap()
    wvr = nc.dram_tensor("wvr", [D, D], F32, kind="ExternalInput").ap()
    wor = nc.dram_tensor("wor", [D, D], F32, kind="ExternalInput").ap()
    wg = nc.dram_tensor("wg", [D, E], F32, kind="ExternalInput").ap()
    w1s = nc.dram_tensor("w1s", [2, D, F_FF], F32, kind="ExternalInput").ap()
    w2s = nc.dram_tensor("w2s", [2, F_FF, D], F32, kind="ExternalInput").ap()
    coremask = nc.dram_tensor("coremask", [8, 1], F32, kind="ExternalInput").ap()
    corec = nc.dram_tensor("corec", [128, 1], F32, kind="ExternalInput").ap()

    ln_dram = nc.dram_tensor("ln_dram", [T, D], BF16)
    ao_dram = nc.dram_tensor("ao_dram", [T, D], BF16)
    lnall_c = [nc.dram_tensor(f"lnall{i}", [8 * T, D], BF16) for i in range(2)]
    counts_c = [nc.dram_tensor(f"counts_d{i}", [1, E], F32) for i in range(2)]
    agcounts_c = [nc.dram_tensor(f"agcounts_d{i}", [8, E], F32) for i in range(2)]
    idtab_c = [nc.dram_tensor(f"idtab{i}", [E * CAP + 128, 1], F32) for i in range(2)]
    idall_c = [nc.dram_tensor(f"idall{i}", [E * CAP + 128, 1], F32) for i in range(2)]
    xe_c = [nc.dram_tensor(f"xe_dram{i}", [2 * CAP, D], BF16) for i in range(2)]
    eo_c = [nc.dram_tensor(f"eo_dram{i}", [2 * CAP, D], BF16) for i in range(2)]
    expall_c = [nc.dram_tensor(f"expall{i}", [E * CAP, D], BF16) for i in range(2)]

    def c_bf(a):
        return np.asarray(a, np.float32).astype(BF)

    stair = np.zeros((128, 31), np.float32)
    stair[:, 15] = 1.0
    rs = np.zeros((16, 16 * 128), np.float32)
    for i in range(16):
        rs[i, i * 128:(i + 1) * 128] = 1.0
    jl, tl = np.meshgrid(np.arange(128), np.arange(128), indexing="ij")
    consts_np = {
        "tri128i_b": c_bf(np.triu(np.ones((128, 128)))),
        "tri128e_f": np.triu(np.ones((128, 128), np.float32), 1),
        "tri128e_b": c_bf(np.triu(np.ones((128, 128)), 1)),
        "tri16e_b": c_bf(np.triu(np.ones((16, 16)), 1)),
        "tri16e_f": np.triu(np.ones((16, 16), np.float32), 1),
        "stair_b": c_bf(stair),
        "stair_f": stair,
        "rs_b": c_bf(rs),
        "rs_f": rs,
        "ones16c_f": np.ones((16, 1), np.float32),
        "ones1x128_b": c_bf(np.ones((1, 128))),
        "ones1x128_f": np.ones((1, 128), np.float32),
        "ones1xT_b": c_bf(np.ones((1, 1024))),
        "onescol_b": c_bf(np.ones((128, 1))),
        "iota16_f": np.broadcast_to(np.arange(16, dtype=np.float32), (128, 16)).copy(),
        "pidx_f": np.arange(128, dtype=np.float32).reshape(128, 1),
        "mask_i": c_bf((jl == tl).astype(np.float32)),
        "mask_prev": c_bf(np.isin(jl - tl, [0, 64, 96, 112, 120, 124, 126, 127]).astype(np.float32)),
        "mask_cur": c_bf(np.isin(tl - jl, [0, 1, 2, 4, 8, 16, 32, 64]).astype(np.float32)),
    }
    const_dram = {k: nc.inline_tensor(v, name="c_" + k) for k, v in consts_np.items()}

    sb = nc.alloc_sbuf_tensor
    xsb = sb("xsb", [128, NT, D], F32)
    C = {}
    for k, v in consts_np.items():
        if k in ("rs_f", "tri128e_f"):
            continue
        C[k] = sb("s_" + k, list(v.shape), BF16 if v.dtype == BF else F32)
    scal = sb("scal", [128, 8], F32)
    bqkv_b = sb("bqkv_b", [1, 3 * D], BF16)
    bproj_b = sb("bproj_b", [1, D], BF16)
    wqk_b = sb("wqk_b", [128, 8, 32], BF16)
    wg_b = sb("wg_b", [128, 8, E], BF16)
    cmask_s = sb("cmask_s", [8, 1], F32)
    corec_s = sb("corec_s", [128, 1], F32)
    idxf_s = sb("idxf_s", [128, NT], F32)
    gsv_s = sb("gsv_s", [128, NT], F32)
    slot2_s = sb("slot2_s", [128, NT], I32)
    oneh_s = sb("oneh_s", [128, NT, E], F32)

    for k in C:
        p.op("sync", dma(C[k][:], const_dram[k].ap()[:, :]), incs=["ldc"], dma=True)
    p.op("sync", dma(cmask_s[:], coremask[:, :]), incs=["ldc"], dma=True)
    p.op("sync", dma(corec_s[:], corec[:, :]), incs=["ldc"], dma=True)
    p.op("gpsimd", dma(bqkv_b[:], bqkv[:, :]), incs=["ldg"], dma=True)
    p.op("gpsimd", dma(bproj_b[:], bproj[:, :]), incs=["ldg"], dma=True)
    p.op("gpsimd", dma(wqk_b[:], wqk.rearrange("(a p) c -> p a c", p=128)),
         incs=["ldg"], dma=True)
    t_ldg = p.op("gpsimd", dma(wg_b[:], wg.rearrange("(a p) c -> p a c", p=128)),
                 incs=["ldg"], dma=True)
    t_ldc = ("sync:ldc", p.sems["sync:ldc"][1])

    p.op("vector", vop("memset", scal[:, 0:1], 1e-5))
    p.op("vector", vop("memset", scal[:, 1:2], float(np.pi / 2)))
    p.op("vector", vop("memset", scal[:, 2:3], 1e-6))
    tk_scal = p.token("vector", "scal")

    tk_x = []
    for i in range(NT):
        tk_x.append(p.op("sync", dma(xsb[:, i, :], x_in[i * 128:(i + 1) * 128, :]),
                         incs=["ldx"], dma=True))

    psc = [0]
    sbc = [0]

    def mk_psum(stack, shape, dt=F32):
        psc[0] += 1
        return stack.enter_context(nc.psum_tensor(f"ps{psc[0]}", shape, dt))

    def mk_sb(stack, shape, dt):
        sbc[0] += 1
        return stack.enter_context(nc.sbuf_tensor(f"sb{sbc[0]}", shape, dt))

    def mm(out, lhsT, rhs, start, stop):
        def fn(eng):
            n = rhs.shape[-1]
            if n <= 512:
                return eng.matmul(out, lhsT, rhs, start=start, stop=stop)
            inst = None
            for c0 in range(0, n, 512):
                c1 = min(c0 + 512, n)
                inst = eng.matmul(out[..., c0:c1], lhsT, rhs[..., c0:c1],
                                  start=start, stop=stop)
            return inst
        return fn

    def ln_pass(stack, extra_wait=()):
        lnb = mk_sb(stack, [128, 2, D], BF16)
        sqs = mk_sb(stack, [128, 1, D], BF16)
        st8 = mk_sb(stack, [128, 2, 8], F32)
        toks = []
        prev = [None, None]
        for i in range(NT):
            b = i % 2
            w = list(extra_wait) + [tk_scal, prev[b]]
            t_sq = p.op("scalar", vop("activation", sqs[:, 0, :], xsb[:, i, :], AF.Square,
                                      accum_out=st8[:, b, 1:2]), waits=w, incs=["lnsq"])
            p.op("vector", vop("reduce_sum", st8[:, b, 0:1], xsb[:, i, :], axis=AX.X),
                 waits=w)
            p.op("vector", vop("tensor_scalar", out=st8[:, b, 2:3], in0=st8[:, b, 0:1],
                               scalar1=1.0 / D, scalar2=None, op0=ALU.mult), waits=[t_sq])
            p.op("vector", vop("tensor_scalar", out=st8[:, b, 3:4], in0=st8[:, b, 2:3],
                               scalar1=-1.0, scalar2=None, op0=ALU.mult))
            p.op("vector", vop("tensor_tensor", out=st8[:, b, 4:5], in0=st8[:, b, 2:3],
                               in1=st8[:, b, 2:3], op=ALU.mult))
            p.op("vector", vop("tensor_scalar", out=st8[:, b, 5:6], in0=st8[:, b, 1:2],
                               scalar1=1.0 / D, scalar2=None, op0=ALU.mult))
            p.op("vector", vop("tensor_tensor", out=st8[:, b, 4:5], in0=st8[:, b, 5:6],
                               in1=st8[:, b, 4:5], op=ALU.subtract))
            t_var = p.token("vector", "lnv")
            t_sd = p.op("scalar", vop("activation", st8[:, b, 6:7], st8[:, b, 4:5],
                                      AF.Sqrt, bias=scal[:, 0:1]), waits=[t_var], incs=["lnsd"])
            p.op("vector", vop("reciprocal", st8[:, b, 7:8], st8[:, b, 6:7]), waits=[t_sd])
            t_r = p.token("vector", "lnv")
            p.op("vector", vop("tensor_scalar", out=lnb[:, b, :], in0=xsb[:, i, :],
                               scalar1=st8[:, b, 3:4], scalar2=st8[:, b, 7:8],
                               op0=ALU.add, op1=ALU.mult), waits=[t_r])
            t_ln = p.token("vector", "lnv")
            tst = p.op("sync", dma(ln_dram.ap()[i * 128:(i + 1) * 128, :], lnb[:, b, :]),
                       waits=[t_ln], incs=[f"lnst{b}"], dma=True)
            prev[b] = tst
            toks.append(tst)
        return toks

    def load_T_into(bufT, src_dram, wait_toks, rows=None):
        toks = []
        for F_ in range(8):
            srcap = src_dram.ap() if rows is None else src_dram.ap()[rows[0]:rows[1], :]
            toks.append(p.op("sync", dmat(bufT[:, F_, :],
                                          srcap[:, F_ * 128:(F_ + 1) * 128]),
                             waits=wait_toks, incs=["ldT"], dma=True))
        return toks

    def load_T(stack, src_dram, wait_toks, width):
        bufT = mk_sb(stack, [128, 8, width], BF16)
        return bufT, load_T_into(bufT, src_dram, wait_toks)

    def out_proj(stack, w_dram, bias_sb, src_toks, add_bias):
        aoT, aoT_toks = load_T(stack, ao_dram, src_toks, T)
        wpr = mk_sb(stack, [128, 8, D], BF16)
        t_w = p.op("gpsimd", dma(wpr[:], w_dram.rearrange("(a p) c -> p a c", p=128)),
                   waits=[src_toks[-1]], incs=["wprld"], dma=True)
        ps_pr = mk_psum(stack, [128, 2, D], F32)
        prev = [None, None]
        for i in range(NT):
            b = i % 2
            w = ([t_w] + aoT_toks if i < 2 else []) + [prev[b]]
            for dc in range(8):
                p.op("tensor", mm(ps_pr[:, b, :], aoT[:, dc, i * 128:(i + 1) * 128],
                                  wpr[:, dc, :], start=(dc == 0),
                                  stop=(dc == 7 and not add_bias)),
                     waits=w if dc == 0 else [])
            if add_bias:
                p.op("tensor", mm(ps_pr[:, b, :], C["ones1x128_b"][:], bias_sb[:, :],
                                  start=False, stop=True))
            t_mm = p.token("tensor", "prmm")
            p.op("vector", vop("tensor_tensor", out=xsb[:, i, :], in0=xsb[:, i, :],
                               in1=ps_pr[:, b, :], op=ALU.add), waits=[t_mm])
            prev[b] = p.token("vector", "pradd")

    def sublayer_attn_a():
        stack = contextlib.ExitStack()
        with stack:
            lnT = mk_sb(stack, [128, 8, T], BF16)
            ln_stack = contextlib.ExitStack()
            ln_toks = ln_pass(ln_stack, extra_wait=[tk_x[-1], t_ldc, t_ldg])
            lnT_toks = load_T_into(lnT, ln_dram, ln_toks)
            ln_stack.close()
            p.barrier("a1ln")
            if STAGE[0] < 2:
                return

            ao_toks = []
            mask_of = {8: "mask_i", 4: "mask_i", 2: "mask_i", 1: "mask_prev", 0: "mask_cur"}
            for hg in range(2):
                hstack = contextlib.ExitStack()
                with hstack:
                    qP = mk_sb(hstack, [128, 4, T], BF16)
                    kz = mk_sb(hstack, [128, 8, T], BF16)
                    wch = mk_sb(hstack, [128, 2, 8, 128], BF16)
                    p.op("vector", vop("memset", kz[:], 0.0))
                    t_kz0 = p.token("vector")
                    qk_stack = contextlib.ExitStack()
                    pss = [mk_psum(qk_stack, [128, 512], F32),
                           mk_psum(qk_stack, [128, 512], F32)]
                    ev = [None, None]
                    for fcl in range(8):
                        wb = fcl % 2
                        col0 = hg * 512 + fcl * 128 if fcl < 4 else 1024 + hg * 512 + (fcl - 4) * 128
                        t_wl = p.op("gpsimd", dma(
                            wch[:, wb, :, :],
                            wqkv.rearrange("(a p) c -> p a c", p=128)[:, :, col0:col0 + 128]),
                            waits=[ev[0], ev[1]] if fcl >= 2 else [], incs=[f"wqkl{wb}"], dma=True)
                        for g in range(4):
                            pb = (fcl * 4 + g) % 2
                            w = [t_wl, ev[pb]] + (lnT_toks if fcl == 0 and g == 0 else [])
                            for dc in range(8):
                                p.op("tensor", mm(pss[pb][:], wch[:, wb, dc, :],
                                                  lnT[:, dc, g * 512:(g + 1) * 512],
                                                  start=(dc == 0), stop=False),
                                     waits=w if dc == 0 else [])
                            p.op("tensor", mm(pss[pb][:], bqkv_b[0:1, col0:col0 + 128],
                                              C["ones1xT_b"][0:1, 0:512], start=False, stop=True))
                            t_mm = p.token("tensor", "qkmm")
                            if fcl < 4:
                                ev[pb] = p.op("scalar", vop("copy",
                                                            out=qP[:, fcl, g * 512:(g + 1) * 512],
                                                            in_=pss[pb][:]),
                                              waits=[t_mm], incs=["qkev"])
                            else:
                                p.op("scalar", vop("copy",
                                                   out=kz[0:64, (fcl - 4) * 2, g * 512:(g + 1) * 512],
                                                   in_=pss[pb][0:64, :]),
                                     waits=[t_mm, t_kz0])
                                ev[pb] = p.op("scalar", vop("copy",
                                                            out=kz[64:128, (fcl - 4) * 2 + 1, g * 512:(g + 1) * 512],
                                                            in_=pss[pb][64:128, :]),
                                              incs=["qkev"])
                    last_qk = p.token("scalar")
                    qk_stack.close()
                    if STAGE[0] < 3:
                        p.barrier(f"dbg{hg}")
                        continue

                    v1h = mk_sb(hstack, [128, 10, 512], BF16)
                    vch = mk_sb(hstack, [128, 8, 512], BF16)
                    em = mk_sb(hstack, [128, 5, 8, 128], BF16)
                    aob = mk_sb(hstack, [128, 2, 512], BF16)
                    zr = mk_sb(hstack, [128, 2, 8], F32)
                    ps_v = mk_psum(hstack, [128, 2, 512], F32)
                    ps_sc = mk_psum(hstack, [128, 2, 512], F32)
                    ps_av = mk_psum(hstack, [128, 2, 512], F32)
                    ps_z = mk_psum(hstack, [128, 2, 8], F32)
                    t_wv = p.op("gpsimd", dma(
                        vch[:],
                        wqkv.rearrange("(a p) c -> p a c", p=128)[:, :, 2048 + hg * 512:2048 + (hg + 1) * 512]),
                        incs=[f"wvl{hg}"], dma=True)
                    v_done = [None] * NT
                    slot_reader = [None] * 10
                    ev_v = [None, None]
                    sc_ev = [None, None]
                    av_ev = [None, None]
                    msk_prev = None
                    for i in range(NT):
                        b = i % 2
                        sl = i % 10
                        w = ([t_wv] + lnT_toks + [last_qk] if i < 2 else []) + \
                            [ev_v[b], slot_reader[sl]]
                        for dc in range(8):
                            p.op("tensor", mm(ps_v[:, b, :],
                                              lnT[:, dc, i * 128:(i + 1) * 128],
                                              vch[:, dc, :], start=(dc == 0), stop=False),
                                 waits=w if dc == 0 else [])
                        p.op("tensor", mm(ps_v[:, b, :], C["ones1x128_b"][:],
                                          bqkv_b[0:1, 2048 + hg * 512:2048 + (hg + 1) * 512],
                                          start=False, stop=True))
                        t_vm = p.token("tensor", "vmm")
                        tve = p.op("scalar", vop("copy", out=v1h[:, sl, :], in_=ps_v[:, b, :]),
                                   waits=[t_vm], incs=["vev"])
                        ev_v[b] = tve
                        v_done[i] = tve
                        if STAGE[0] < 4:
                            slot_reader[sl] = tve
                            continue
                        if STAGE[0] == 42:
                            pass

                        chunks = [(ci, off) for ci, off in enumerate(SHIFT_OFFS) if i - off >= 0]
                        pairs = [(ci, off, h) for (ci, off) in chunks for h in range(8)]
                        em_groups = []
                        for gi in range(0, len(pairs), 4):
                            grp = pairs[gi:gi + 4]
                            pb = (gi // 4) % 2
                            w = [sc_ev[pb], msk_prev]
                            for k_, (ci, off, h) in enumerate(grp):
                                c = i - off
                                p.op("tensor", mm(ps_sc[:, pb, k_ * 128:(k_ + 1) * 128],
                                                  kz[:, h, c * 128:(c + 1) * 128],
                                                  qP[:, h // 2, i * 128:(i + 1) * 128],
                                                  start=True, stop=True),
                                     waits=w if k_ == 0 else [])
                            t_g = p.token("tensor", "scmm")
                            base = grp[0][0] * 8 + grp[0][2]
                            if STAGE[0] == 41:
                                t_e = p.op("scalar", vop(
                                    "copy",
                                    out=em.rearrange("p c h f -> p (c h) f")[:, base:base + len(grp), :],
                                    in_=ps_sc[:, pb, 0:len(grp) * 128]),
                                    waits=[t_g], incs=["scexp"])
                            else:
                                t_e = p.op("scalar", vop(
                                    "activation",
                                    em.rearrange("p c h f -> p (c h) f")[:, base:base + len(grp), :],
                                    ps_sc.rearrange("p b (k f) -> p b k f", f=128)[:, pb, 0:len(grp), :],
                                    AF.Exp, scale=0.125),
                                    waits=[t_g], incs=["scexp"])
                            sc_ev[pb] = t_e
                            em_groups.append((t_e, grp))
                        if STAGE[0] < 50:
                            msk_prev = sc_ev[0]
                            av_ev[b] = em_groups[-1][0]
                            continue
                        mk_toks = []
                        for (t_e, grp) in em_groups:
                            runs = []
                            for (ci, off, h) in grp:
                                if runs and runs[-1][0] == ci:
                                    runs[-1][2] += 1
                                else:
                                    runs.append([ci, h, 1])
                            for (ci, h0, n) in runs:
                                off = SHIFT_OFFS[ci]
                                p.op("vector", vop("tensor_tensor",
                                                   out=em[:, ci, h0:h0 + n, :],
                                                   in0=em[:, ci, h0:h0 + n, :],
                                                   in1=C[mask_of[off]][:, None, :].broadcast_to([128, n, 128]),
                                                   op=ALU.mult), waits=[t_e])
                            mk_toks.append(p.token("vector", "msk"))
                        msk_prev = mk_toks[-1]
                        w_av = [av_ev[b]] + mk_toks + [v_done[i - off] for (ci, off) in chunks]
                        first = True
                        for h in range(8):
                            for (ci, off) in chunks:
                                p.op("tensor", mm(ps_av[:, b, h * 64:(h + 1) * 64],
                                                  em[:, ci, h, :],
                                                  v1h[:, (i - off) % 10, h * 64:(h + 1) * 64],
                                                  start=(ci == chunks[0][0]),
                                                  stop=(ci == chunks[-1][0])),
                                     waits=w_av if first else [])
                                first = False
                            for (ci, off) in chunks:
                                p.op("tensor", mm(ps_z[:, b, h:h + 1], em[:, ci, h, :],
                                                  C["onescol_b"][:],
                                                  start=(ci == chunks[0][0]),
                                                  stop=(ci == chunks[-1][0])))
                        t_av = p.token("tensor", "avmm")
                        for (ci, off) in chunks:
                            slot_reader[(i - off) % 10] = t_av
                        t_zr = p.op("vector", vop("reciprocal", zr[:, b, :], ps_z[:, b, :]),
                                    waits=[t_av], incs=["zrec"])
                        p.op("vector", vop("tensor_tensor",
                                           out=aob.rearrange("p b (h e) -> p b h e", e=64)[:, b],
                                           in0=ps_av.rearrange("p b (h e) -> p b h e", e=64)[:, b],
                                           in1=zr[:, b, :, None].broadcast_to([128, 8, 64]),
                                           op=ALU.mult), waits=[t_zr])
                        t_ao = p.token("vector", "aon")
                        t_aos = p.op("sync", dma(
                            ao_dram.ap()[i * 128:(i + 1) * 128, hg * 512:(hg + 1) * 512],
                            aob[:, b, :]), waits=[t_ao], incs=[f"aost{b}"], dma=True)
                        av_ev[b] = t_aos
                        ao_toks.append(t_aos)
                p.barrier(f"a1h{hg}")
        if STAGE[0] < 6 or not ao_toks:
            return
        stack = contextlib.ExitStack()
        with stack:
            out_proj(stack, wproj, bproj_b, ao_toks, add_bias=True)
        p.barrier("a1")

    def sublayer_moe(call):
        MS = MSTAGE[0] if call == 1 else 99
        lnall = lnall_c[call]
        counts_d = counts_c[call]
        agcounts_d = agcounts_c[call]
        idtab = idtab_c[call]
        idall = idall_c[call]
        xe_dram = xe_c[call]
        eo_dram = eo_c[call]
        expall = expall_c[call]
        stack = contextlib.ExitStack()
        with stack:
            rs_f = mk_sb(stack, [16, 16 * 128], F32)
            tri128e_f = mk_sb(stack, [128, 128], F32)
            p.op("sync", dma(rs_f[:], const_dram["rs_f"].ap()[:, :]), incs=["ldT"], dma=True)
            t_mc = p.op("sync", dma(tri128e_f[:], const_dram["tri128e_f"].ap()[:, :]),
                        incs=["ldT"], dma=True)
            ln_toks = ln_pass(stack)
            t_ag = p.op("gpsimd", collective("AllGather", ALU.bypass,
                                             [ln_dram.ap().opt()], [lnall.ap().opt()]),
                        waits=ln_toks, incs=[f"agd{call}"], cc=True)
            if MS < 22:
                p.barrier(f"mdbg{call}")
                return
            lnT, lnT_toks = load_T(stack, ln_dram, ln_toks, T)
            if MS < 23:
                p.barrier(f"mdbg{call}")
                return

            ps_lg = mk_psum(stack, [128, 2, E], F32)
            gsc = mk_sb(stack, [128, 2, 4], F32)
            eg = mk_sb(stack, [128, 2, E], F32)
            lg_ev = [None, None]
            oneh_toks = []
            for i in range(NT):
                b = i % 2
                w = (lnT_toks + [t_ldg, t_ldc] if i < 2 else []) + [lg_ev[b]]
                for dc in range(8):
                    p.op("tensor", mm(ps_lg[:, b, :], lnT[:, dc, i * 128:(i + 1) * 128],
                                      wg_b[:, dc, :], start=(dc == 0), stop=(dc == 7)),
                         waits=w if dc == 0 else [])
                t_lg = p.token("tensor", "lgmm")
                if MS < 24:
                    lg_ev[b] = t_lg
                    oneh_toks.append(t_lg)
                    continue
                p.op("vector", vop("reduce_max", gsc[:, b, 3:4], ps_lg[:, b, :], axis=AX.X),
                     waits=[t_lg])
                p.op("vector", vop("tensor_tensor", out=eg[:, b, :], in0=ps_lg[:, b, :],
                                   in1=gsc[:, b, 3:4].broadcast_to([128, E]),
                                   op=ALU.is_equal))
                p.op("vector", vop("tensor_scalar", out=eg[:, b, :], in0=eg[:, b, :],
                                   scalar1=-1.0, scalar2=-1e9, op0=ALU.add, op1=ALU.mult))
                p.op("vector", vop("tensor_tensor", out=eg[:, b, :], in0=eg[:, b, :],
                                   in1=C["iota16_f"][:], op=ALU.add))
                p.op("vector", vop("tensor_reduce", out=idxf_s[:, i:i + 1], in_=eg[:, b, :],
                                   axis=AX.X, op=ALU.min))
                p.op("vector", vop("tensor_scalar", out=gsc[:, b, 0:1], in0=gsc[:, b, 3:4],
                                   scalar1=-1.0, scalar2=None, op0=ALU.mult))
                t_nm = p.token("vector", "gv")
                t_ex = p.op("scalar", vop("activation", eg[:, b, :], ps_lg[:, b, :], AF.Exp,
                                          bias=gsc[:, b, 0:1], accum_out=gsc[:, b, 1:2]),
                            waits=[t_nm], incs=["gexp"])
                p.op("vector", vop("reciprocal", gsc[:, b, 2:3], gsc[:, b, 1:2]),
                     waits=[t_ex])
                p.op("vector", vop("tensor_copy", out=gsv_s[:, i:i + 1], in_=gsc[:, b, 2:3]))
                p.op("vector", vop("tensor_tensor", out=oneh_s[:, i, :],
                                   in0=C["iota16_f"][:],
                                   in1=idxf_s[:, i:i + 1].broadcast_to([128, E]),
                                   op=ALU.is_equal))
                t_oh = p.token("vector", "gv")
                lg_ev[b] = t_oh
                oneh_toks.append(t_oh)


            if MS < 25:
                p.barrier(f"mdbg{call}")
                return
            ps_tm = mk_psum(stack, [16, E], F32)
            for i in range(NT):
                p.op("tensor", mm(ps_tm[:], C["stair_f"][:, 15 - i:31 - i], oneh_s[:, i, :],
                                  start=(i == 0), stop=(i == NT - 1)),
                     waits=[oneh_toks[i]])
            t_tm = p.token("tensor", "tmm")

            tmsb = mk_sb(stack, [16, E], F32)
            offm = mk_sb(stack, [16, E], F32)
            crow = mk_sb(stack, [1, 2, E], F32)
            bco = mk_sb(stack, [128, E], F32)
            agc = mk_sb(stack, [8, E], F32)
            t_te = p.op("vector", vop("tensor_copy", out=tmsb[:], in_=ps_tm[:]),
                        waits=[t_tm], incs=["gv"])
            ps_o16 = mk_psum(stack, [16, E], F32)
            t_om = p.op("tensor", mm(ps_o16[:], C["tri16e_f"][:], tmsb[:], True, True),
                        waits=[t_te], incs=["omm"])
            t_of = p.op("vector", vop("tensor_copy", out=offm[:], in_=ps_o16[:]),
                        waits=[t_om], incs=["gv"])
            offh = mk_sb(stack, [16, E], BF16)
            offl = mk_sb(stack, [16, E], BF16)
            p.op("vector", vop("tensor_copy", out=offh[:], in_=offm[:]))
            p.op("vector", vop("tensor_tensor", out=offl[:], in0=offm[:],
                               in1=offh[:], op=ALU.subtract))
            t_ofhl = p.token("vector")
            ps_c = mk_psum(stack, [1, 2, E], F32)
            t_cm = p.op("tensor", mm(ps_c[:, 0, :], C["ones16c_f"][:], tmsb[:], True, True),
                        waits=[t_te], incs=["cmm"])
            t_ce = p.op("vector", vop("tensor_copy", out=crow[:, 0, :], in_=ps_c[:, 0, :]),
                        waits=[t_cm], incs=["gv"])
            t_cst = p.op("sync", dma(counts_d.ap()[:, :], crow[:, 0, :]),
                         waits=[t_ce], incs=["cst"], dma=True)
            t_cag = p.op("gpsimd", collective("AllGather", ALU.bypass,
                                              [counts_d.ap().opt()], [agcounts_d.ap().opt()]),
                         waits=[t_cst], incs=[f"agc{call}"], cc=True)
            t_agl = p.op("sync", dma(agc[:], agcounts_d.ap()[:, :]),
                         waits=[t_cag], incs=["agl"], dma=True)
            t_co = p.op("tensor", mm(ps_c[:, 1, :], cmask_s[:], agc[:], True, True),
                        waits=[t_agl], incs=["comm"])
            t_coe = p.op("vector", vop("tensor_copy", out=crow[:, 1, :], in_=ps_c[:, 1, :]),
                         waits=[t_co], incs=["gv"])
            ps_bc = mk_psum(stack, [128, E], F32)
            t_bc = p.op("tensor", mm(ps_bc[:], C["ones1x128_f"][:], crow[:, 1, :], True, True),
                        waits=[t_coe], incs=["bcmm"])
            t_bce = p.op("vector", vop("tensor_copy", out=bco[:], in_=ps_bc[:]),
                         waits=[t_bc], incs=["gv"])

            if MS < 3:
                p.barrier(f"mdbg{call}")
                return
            ps_rks = [mk_psum(stack, [128, E], F32), mk_psum(stack, [128, E], F32)]
            oneb = mk_sb(stack, [128, 2, E], BF16)
            lsc = mk_sb(stack, [128, 2, E], F32)
            l8 = mk_sb(stack, [128, 2, 8], F32)
            sl32 = mk_sb(stack, [128, 2, 1], I32)
            zt = mk_sb(stack, [128, 128], F32)
            dumpslot = mk_sb(stack, [128, 1], F32)
            p.op("vector", vop("memset", zt[:], 0.0))
            p.op("vector", vop("tensor_scalar", out=dumpslot[:], in0=C["pidx_f"][:],
                               scalar1=float(E * CAP), scalar2=None, op0=ALU.add))
            t_z = p.token("vector", "gv")
            t_zt0 = p.op("sync", dma(idtab.ap()[0:E * CAP].rearrange("(p a) c -> p (a c)", p=128),
                                     zt[:]), waits=[t_z], incs=["ztab"], dma=True)
            t_zt = p.op("sync", dma(idtab.ap()[E * CAP:E * CAP + 128].rearrange("(p a) c -> p (a c)", p=128),
                                    zt[:, 0:1]), waits=[t_z], incs=["ztab"], dma=True)
            rk_ev = [None, None]
            scat_toks = [t_zt]
            for i in range(NT):
                b = i % 2
                t_ob = p.op("vector", vop("tensor_copy", out=oneb[:, b, :],
                                          in_=oneh_s[:, i, :]), waits=[rk_ev[b]])
                w = [t_bce, t_ofhl, t_ob]
                p.op("tensor", mm(ps_rks[b][:], C["tri128e_b"][:], oneb[:, b, :],
                                  True, False), waits=w)
                p.op("tensor", mm(ps_rks[b][:], C["rs_b"][:, i * 128:(i + 1) * 128],
                                  offh[:], False, False))
                t_rk = p.op("tensor", mm(ps_rks[b][:], C["rs_b"][:, i * 128:(i + 1) * 128],
                                         offl[:], False, True), incs=["rkmm"])
                p.op("vector", vop("tensor_tensor", out=lsc[:, b, :], in0=ps_rks[b][:],
                                   in1=bco[:], op=ALU.add), waits=[t_rk])
                p.op("vector", vop("tensor_tensor", out=lsc[:, b, :], in0=lsc[:, b, :],
                                   in1=oneh_s[:, i, :], op=ALU.mult))
                p.op("vector", vop("reduce_sum", l8[:, b, 0:1], lsc[:, b, :], axis=AX.X))
                p.op("vector", vop("tensor_scalar", out=l8[:, b, 1:2], in0=l8[:, b, 0:1],
                                   scalar1=float(CAP), scalar2=None, op0=ALU.is_lt))
                p.op("vector", vop("tensor_scalar", out=l8[:, b, 2:3], in0=l8[:, b, 1:2],
                                   scalar1=-1.0, scalar2=-1.0, op0=ALU.add, op1=ALU.mult))
                p.op("vector", vop("tensor_tensor", out=l8[:, b, 2:3], in0=l8[:, b, 2:3],
                                   in1=dumpslot[:], op=ALU.mult))
                p.op("vector", vop("tensor_scalar", out=l8[:, b, 3:4], in0=idxf_s[:, i:i + 1],
                                   scalar1=float(CAP), scalar2=None, op0=ALU.mult))
                p.op("vector", vop("tensor_tensor", out=l8[:, b, 4:5], in0=l8[:, b, 3:4],
                                   in1=l8[:, b, 0:1], op=ALU.add))
                p.op("vector", vop("tensor_tensor", out=l8[:, b, 4:5], in0=l8[:, b, 4:5],
                                   in1=l8[:, b, 1:2], op=ALU.mult))
                p.op("vector", vop("tensor_tensor", out=l8[:, b, 4:5], in0=l8[:, b, 4:5],
                                   in1=l8[:, b, 2:3], op=ALU.add))
                if MSTAGE[0] == 312:
                    rk_ev[b] = p.token("vector")
                    scat_toks.append(rk_ev[b])
                    continue
                p.op("vector", vop("tensor_copy", out=sl32[:, b, :], in_=l8[:, b, 4:5]))
                p.op("vector", vop("tensor_scalar", out=l8[:, b, 5:6], in0=l8[:, b, 0:1],
                                   scalar1=float(CAP - 1), scalar2=None, op0=ALU.min))
                p.op("vector", vop("tensor_tensor", out=l8[:, b, 5:6], in0=l8[:, b, 5:6],
                                   in1=l8[:, b, 3:4], op=ALU.add))
                p.op("vector", vop("tensor_copy", out=slot2_s[:, i:i + 1], in_=l8[:, b, 5:6]))
                p.op("vector", vop("tensor_tensor", out=gsv_s[:, i:i + 1],
                                   in0=gsv_s[:, i:i + 1], in1=l8[:, b, 1:2], op=ALU.mult))
                p.op("vector", vop("tensor_tensor", out=l8[:, b, 6:7], in0=C["pidx_f"][:],
                                   in1=corec_s[:], op=ALU.add))
                p.op("vector", vop("tensor_scalar", out=l8[:, b, 6:7], in0=l8[:, b, 6:7],
                                   scalar1=float(i * 128), scalar2=None, op0=ALU.add))
                t_v = p.token("vector", "gv")
                if MS >= 32:
                    t_s = p.op("gpsimd", iscatter(idtab.ap()[:, :], sl32[:, b, :],
                                                  l8[:, b, 6:7]),
                               waits=[t_v, t_zt, scat_toks[-1] if len(scat_toks) > 1 else None],
                               incs=["scat"], dma=True)
                else:
                    t_s = t_v
                rk_ev[b] = t_s
                scat_toks.append(t_s)
            if MS < 33:
                p.barrier(f"mdbg{call}")
                return
            t_ar = p.op("gpsimd", collective("AllReduce", ALU.add,
                                             [idtab.ap().opt()], [idall.ap().opt()]),
                        waits=scat_toks, incs=[f"arid{call}"], cc=True)

            if MS < 4:
                p.barrier(f"mdbg{call}")
                return
            gidb = mk_sb(stack, [128, 1], F32)
            gid32 = mk_sb(stack, [128, 1], I32)
            idv = mk_sb(stack, [128, 16], F32)
            idv32 = mk_sb(stack, [128, 16], I32)
            p.op("vector", vop("tensor_scalar", out=gidb[:], in0=C["pidx_f"][:],
                               scalar1=16.0, scalar2=None, op0=ALU.mult))
            p.op("vector", vop("tensor_tensor", out=gidb[:], in0=gidb[:],
                               in1=corec_s[:], op=ALU.add))
            p.op("vector", vop("tensor_copy", out=gid32[:], in_=gidb[:]))
            t_gid = p.token("vector", "gv")
            t_idl = p.op("gpsimd", igather(idv[:], idall.ap()[:, :], gid32[:]),
                         waits=[t_ar, t_gid], incs=["idl"], dma=True)
            t_iv = p.op("vector", vop("tensor_copy", out=idv32[:], in_=idv[:]),
                        waits=[t_idl], incs=["gv"])
            gath_toks = []
            xe_r = xe_dram.ap().rearrange("(a b) c -> a b c", b=16)
            xeb = mk_sb(stack, [128, 4, D], BF16)
            xeb_prev = [None] * 4
            for k_ in range(16):
                b_ = k_ % 4
                t_gx = p.op("gpsimd", igather(xeb[:, b_, :], lnall.ap()[:, :],
                                              idv32[:, k_:k_ + 1]),
                            waits=[t_iv, t_ag, xeb_prev[b_]], incs=[f"xeg{b_}"], dma=True)
                t_sx = p.op("sync", dma(xe_r[:, k_, :], xeb[:, b_, :]),
                            waits=[t_gx], incs=[f"xest{b_}"], dma=True)
                xeb_prev[b_] = t_sx
                gath_toks.append(t_sx)

        if MS < 5:
            p.barrier(f"mdbg{call}")
            return
        p.barrier(f"moepre{call}")
        stack = contextlib.ExitStack()
        with stack:
            if MSTAGE[0] in (71, 72, 73):
                safei = mk_sb(stack, [128, 1], I32)
                dummy = mk_sb(stack, [128, D], BF16)
                p.op("vector", vop("tensor_copy", out=safei[:], in_=C["pidx_f"][:]))
                t_si = p.token("vector")
            if MSTAGE[0] == 71:
                p.op("gpsimd", igather(dummy[:], lnall.ap()[:, :], safei[:]),
                     waits=[t_si], incs=["cbg0"], dma=True)
                p.barrier(f"mdbg{call}")
                return
            xeT = mk_sb(stack, [128, 8, CAP], BF16)
            xeT_toks0 = []
            if MSTAGE[0] == 72:
                for F_ in range(8):
                    xeT_toks0.append(p.op("sync", dmat(
                        xeT[:, F_, :],
                        xe_dram.ap()[0:CAP, F_ * 128:(F_ + 1) * 128]),
                        waits=gath_toks, incs=["xeT"], dma=True))
            if MSTAGE[0] == 72:
                p.op("gpsimd", igather(dummy[:], lnall.ap()[:, :], safei[:]),
                     waits=[t_si] + xeT_toks0, incs=["cbg0"], dma=True)
                p.barrier(f"mdbg{call}")
                return
            if MSTAGE[0] in (73, 74, 75, 76):
                safei73 = mk_sb(stack, [128, 1], I32)
                dummy73 = mk_sb(stack, [128, D], BF16)
                p.op("vector", vop("tensor_copy", out=safei73[:], in_=C["pidx_f"][:]))
                t_si73 = p.token("vector")
            hT = mk_sb(stack, [128, 32, 512], BF16)
            w1c = mk_sb(stack, [128, 2, 8, 512], BF16)
            w2c = mk_sb(stack, [128, 2, 4, 1024], BF16)
            eob = mk_sb(stack, [128, 2, 1024], BF16)
            eo_toks = []
            o_ev_g = None
            t_lastmm1 = None
            prev_relu = None
            w2_mm_tok = {}
            prev_ost = [None, None]
            for e_ in range(2):
                xeT_toks = []
                for F_ in range(8):
                    xeT_toks.append(p.op("sync", dmat(
                        xeT[:, F_, :],
                        xe_dram.ap()[e_ * CAP:(e_ + 1) * CAP, F_ * 128:(F_ + 1) * 128]),
                        waits=list(gath_toks) + ([t_lastmm1] if e_ else []),
                        incs=["xeT"], dma=True))
                for th in range(2):
                    ph_stack = contextlib.ExitStack()
                    ps_h = mk_psum(ph_stack, [128, 2, 512], F32)
                    h_ev = [None, None]
                    w1_t = [None, None]
                    relu_done = []
                    for slab in range(8):
                        sb_ = slab % 2
                        w1_t[sb_] = p.op("gpsimd", dma(
                            w1c[:, sb_, :, :],
                            w1s[e_].rearrange("(a p) c -> p a c", p=128)[:, :, slab * 512:(slab + 1) * 512]),
                            waits=[relu_done[-5] if len(relu_done) >= 5 else t_lastmm1],
                            incs=[f"w1l{sb_}"], dma=True)
                        for fq in range(4):
                            fc = slab * 4 + fq
                            b = fc % 2
                            w = [w1_t[sb_], h_ev[b]] + \
                                (xeT_toks + [prev_relu, o_ev_g] if fc == 0 else [])
                            for dc in range(8):
                                p.op("tensor", mm(ps_h[:, b, :],
                                                  w1c[:, sb_, dc, fq * 128:(fq + 1) * 128],
                                                  xeT[:, dc, th * 512:(th + 1) * 512],
                                                  start=(dc == 0), stop=(dc == 7)),
                                     waits=w if dc == 0 else [])
                            t_h = p.token("tensor", "hmm")
                            t_lastmm1 = t_h
                            t_r = p.op("scalar", vop("activation", hT[:, fc, :],
                                                     ps_h[:, b, :], AF.Relu),
                                       waits=[t_h], incs=["relu"])
                            h_ev[b] = t_r
                            relu_done.append(t_r)
                    prev_relu = relu_done[-1]
                    ph_stack.close()
                    if MSTAGE[0] == 73 and e_ == 0 and th == 0:
                        p.op("gpsimd", igather(dummy73[:], lnall.ap()[:, :], safei73[:]),
                             waits=[t_si73, relu_done[-1]], incs=["cbg0"], dma=True)
                        p.barrier(f"mdbg{call}")
                        return
                    po_stack = contextlib.ExitStack()
                    ps_o = [mk_psum(po_stack, [128, 1024], F32) for _ in range(4)]
                    w2_t = [None, None]
                    ost_toks = []
                    for fc in range(32):
                        if fc % 4 == 0:
                            slab = fc // 4
                            wkey = (slab - 2) % 2 if slab >= 2 else None
                            w2_t[slab % 2] = p.op("gpsimd", dma(
                                w2c[:, slab % 2, :, :],
                                w2s[e_].rearrange("(a p) c -> p a c", p=128)[:, slab * 4:(slab + 1) * 4, :]),
                                waits=[w2_mm_tok.get(slab - 2) if slab >= 2 else o_ev_g],
                                incs=[f"w2l{slab % 2}"], dma=True)
                        for tq in range(4):
                            w = []
                            if fc == 0:
                                w = [relu_done[-1], o_ev_g]
                            if fc % 4 == 0 and tq == 0:
                                w = w + [w2_t[(fc // 4) % 2]]
                            p.op("tensor", mm(ps_o[tq][:],
                                              hT[:, fc, tq * 128:(tq + 1) * 128],
                                              w2c[:, (fc // 4) % 2, fc % 4, :],
                                              start=(fc == 0), stop=(fc == 31)),
                                 waits=w)
                        if fc % 4 == 3 and fc != 31:
                            w2_mm_tok[fc // 4] = p.token("tensor", "w2use")
                    t_o = p.token("tensor", "omm2")
                    for tq in range(4):
                        b = tq % 2
                        t_oe = p.op("vector", vop("tensor_copy", out=eob[:, b, :],
                                                  in_=ps_o[tq][:]),
                                    waits=[t_o, prev_ost[b]],
                                    incs=["oev"])
                        row0 = (e_ * 8 + th * 4 + tq) * 128
                        t_os = p.op("sync", dma(eo_dram.ap()[row0:row0 + 128, :],
                                                eob[:, b, :]),
                                    waits=[t_oe], incs=[f"ost{b}"], dma=True)
                        ost_toks.append(t_os)
                        prev_ost[b] = t_os
                        eo_toks.append(t_os)
                    o_ev_g = ost_toks[-1]
                    po_stack.close()
                    if MSTAGE[0] == 74 and e_ == 0 and th == 0:
                        p.op("gpsimd", igather(dummy73[:], lnall.ap()[:, :], safei73[:]),
                             waits=[t_si73, o_ev_g], incs=["cbg0"], dma=True)
                        p.barrier(f"mdbg{call}")
                        return
            if MSTAGE[0] == 75:
                p.op("gpsimd", igather(dummy73[:], lnall.ap()[:, :], safei73[:]),
                     waits=[t_si73] + eo_toks, incs=["cbg0"], dma=True)
                p.barrier(f"mdbg{call}")
                return
            t_eag = p.op("gpsimd", collective("AllGather", ALU.bypass,
                                              [eo_dram.ap().opt()], [expall.ap().opt()]),
                         waits=eo_toks, incs=[f"eag{call}"], cc=True)
            if MSTAGE[0] == 76:
                p.op("gpsimd", igather(dummy73[:], lnall.ap()[:, :], safei73[:]),
                     waits=[t_si73, t_eag], incs=["cbg0"], dma=True)
                p.barrier(f"mdbg{call}")
                return

            if MS < 6:
                p.barrier(f"mdbg{call}")
                return
            got = mk_sb(stack, [128, 4, D], BF16)
            cb_ev = [None, None, None, None]
            if MSTAGE[0] in (65, 66, 67):
                safei = mk_sb(stack, [128, 1], I32)
                p.op("vector", vop("tensor_copy", out=safei[:], in_=C["pidx_f"][:]))
                t_si = p.token("vector")
                srct = {65: expall, 66: lnall, 67: eo_dram}[MSTAGE[0]]
                t_g = p.op("gpsimd", igather(got[:, 0, :], srct.ap()[:, :], safei[:]),
                           waits=[t_eag, t_si], incs=["cbg0"], dma=True)
                p.barrier(f"mdbg{call}")
                return
            for i in range(NT):
                b = i % 4
                t_g = p.op("gpsimd", igather(got[:, b, :], expall.ap()[:, :],
                                             slot2_s[:, i:i + 1]),
                           waits=[t_eag, cb_ev[b]], incs=[f"cbg{b}"], dma=True)
                if MSTAGE[0] == 61:
                    cb_ev[b] = t_g
                    continue
                p.op("vector", vop("scalar_tensor_tensor", out=xsb[:, i, :],
                                   in0=got[:, b, :], scalar=gsv_s[:, i:i + 1],
                                   in1=xsb[:, i, :], op0=ALU.mult, op1=ALU.add),
                     waits=[t_g])
                cb_ev[b] = p.token("vector", "cb")
        p.barrier("moe")

    def sublayer_attn_b():
        stack = contextlib.ExitStack()
        with stack:
            CV = mk_sb(stack, [128, NT, 2, 1040], BF16)
            CQS = mk_sb(stack, [128, NT, 32], BF16)
            bv_stack = contextlib.ExitStack()
            ln_toks = ln_pass(bv_stack)
            lnT, lnT_toks = load_T(bv_stack, ln_dram, ln_toks, T)
            wvc = mk_sb(bv_stack, [128, 8, D], BF16)
            t_wv = p.op("gpsimd", dma(wvc[:], wvr.rearrange("(a p) c -> p a c", p=128)),
                        incs=["wvrl"], dma=True)
            ps_v = mk_psum(bv_stack, [128, 2, 1024], F32)
            ps_q = mk_psum(bv_stack, [128, 2, 32], F32)
            trg = mk_sb(bv_stack, [128, 2, 96], F32)
            cv_toks = []
            v_ev = [None, None]
            CVr = CV[:, :, :, 0:1024].rearrange("p i g (h e) -> p i g h e", e=64)
            for i in range(NT):
                b = i % 2
                w = (lnT_toks + [t_wv, t_ldg] if i < 2 else []) + [v_ev[b]]
                for dc in range(8):
                    p.op("tensor", mm(ps_v[:, b, :], lnT[:, dc, i * 128:(i + 1) * 128],
                                      wvc[:, dc, :], start=(dc == 0), stop=(dc == 7)),
                         waits=w if dc == 0 else [])
                for dc in range(8):
                    p.op("tensor", mm(ps_q[:, b, :], lnT[:, dc, i * 128:(i + 1) * 128],
                                      wqk_b[:, dc, :], start=(dc == 0), stop=(dc == 7)))
                t_mm = p.token("tensor", "bvmm")
                t_th = p.op("scalar", vop("activation", trg[:, b, 0:32], ps_q[:, b, :],
                                          AF.Tanh), waits=[t_mm], incs=["btr"])
                p.op("scalar", vop("activation", trg[:, b, 32:64], trg[:, b, 0:32],
                                   AF.Sin, scale=float(np.pi / 4), bias=scal[:, 1:2]))
                p.op("scalar", vop("activation", trg[:, b, 64:96], trg[:, b, 0:32],
                                   AF.Sin, scale=float(np.pi / 4)))
                t_tr = p.token("scalar", "btr")
                p.op("vector", vop("tensor_copy", out=CQS[:, i, 0:16], in_=trg[:, b, 32:48]),
                     waits=[t_tr])
                p.op("vector", vop("tensor_copy", out=CQS[:, i, 16:32], in_=trg[:, b, 64:80]))
                p.op("vector", vop("tensor_tensor", out=CVr[:, i, 0, :, 0:64],
                                   in0=ps_v.rearrange("p b (h e) -> p b h e", e=64)[:, b],
                                   in1=trg[:, b, 48:64][:, :, None].broadcast_to([128, 16, 64]),
                                   op=ALU.mult), waits=[t_mm])
                p.op("vector", vop("tensor_copy", out=CV[:, i, 0, 1024:1040],
                                   in_=trg[:, b, 48:64]))
                p.op("vector", vop("tensor_tensor", out=CVr[:, i, 1, :, 0:64],
                                   in0=ps_v.rearrange("p b (h e) -> p b h e", e=64)[:, b],
                                   in1=trg[:, b, 80:96][:, :, None].broadcast_to([128, 16, 64]),
                                   op=ALU.mult))
                p.op("vector", vop("tensor_copy", out=CV[:, i, 1, 1024:1040],
                                   in_=trg[:, b, 80:96]))
                t_cv = p.token("vector", "bcv")
                v_ev[b] = t_cv
                cv_toks.append(t_cv)

            bv_stack.close()
            TMh = mk_sb(stack, [16, 2, 1040], BF16)
            TMl = mk_sb(stack, [16, 2, 1040], BF16)
            OFh = mk_sb(stack, [16, 2, 1040], BF16)
            OFl = mk_sb(stack, [16, 2, 1040], BF16)
            of_toks = []
            for g in range(2):
                tg_stack = contextlib.ExitStack()
                ps_t2 = mk_psum(tg_stack, [16, 1040], F32)
                for i in range(NT):
                    p.op("tensor", mm(ps_t2[:, 0:1024], C["stair_b"][:, 15 - i:31 - i],
                                      CV[:, i, g, 0:1024], start=(i == 0),
                                      stop=(i == NT - 1)),
                         waits=[cv_toks[-1]] if i == 0 else [cv_toks[i]])
                for i in range(NT):
                    p.op("tensor", mm(ps_t2[:, 1024:1040], C["stair_b"][:, 15 - i:31 - i],
                                      CV[:, i, g, 1024:1040], start=(i == 0),
                                      stop=(i == NT - 1)))
                t_t2 = p.token("tensor", "btm")
                p.op("vector", vop("tensor_copy", out=TMh[:, g, :], in_=ps_t2[:]),
                     waits=[t_t2])
                p.op("vector", vop("tensor_tensor", out=TMl[:, g, :], in0=ps_t2[:],
                                   in1=TMh[:, g, :], op=ALU.subtract))
                t_tme = p.token("vector", "btme")
                ps_of = mk_psum(tg_stack, [16, 1040], F32)
                p.op("tensor", mm(ps_of[:], C["tri16e_b"][:], TMh[:, g, :], True, False),
                     waits=[t_tme])
                t_of = p.op("tensor", mm(ps_of[:], C["tri16e_b"][:], TMl[:, g, :],
                                         False, True), incs=["bof"])
                p.op("vector", vop("tensor_copy", out=OFh[:, g, :], in_=ps_of[:]),
                     waits=[t_of])
                p.op("vector", vop("tensor_tensor", out=OFl[:, g, :], in0=ps_of[:],
                                   in1=OFh[:, g, :], op=ALU.subtract))
                of_toks.append(p.token("vector", "bofe"))
                tg_stack.close()

            ps_cA = mk_psum(stack, [128, 1040], F32)
            ps_cB = mk_psum(stack, [128, 1040], F32)
            cum = mk_sb(stack, [128, 2, 2, 1040], BF16)
            numb = mk_sb(stack, [128, 2, 1024], BF16)
            tmpb = mk_sb(stack, [128, 2, 1024], BF16)
            den = mk_sb(stack, [128, 2, 2, 16], F32)
            hds = mk_sb(stack, [128, 2, 1024], BF16)
            hd_toks = []
            cum_ev = None
            hd_ev = [None, None]
            for i in range(NT):
                b = i % 2
                w = of_toks + [cum_ev]
                for g, pc in ((0, ps_cA), (1, ps_cB)):
                    for (c0, c1) in ((0, 1024), (1024, 1040)):
                        p.op("tensor", mm(pc[:, c0:c1], C["tri128i_b"][:],
                                          CV[:, i, g, c0:c1], start=True, stop=False),
                             waits=w if (g == 0 and c0 == 0) else [])
                        p.op("tensor", mm(pc[:, c0:c1], C["rs_b"][:, i * 128:(i + 1) * 128],
                                          OFh[:, g, c0:c1], start=False, stop=False))
                        p.op("tensor", mm(pc[:, c0:c1], C["rs_b"][:, i * 128:(i + 1) * 128],
                                          OFl[:, g, c0:c1], start=False, stop=True))
                t_cum = p.token("tensor", "bcum")
                p.op("vector", vop("tensor_copy", out=cum[:, b, 0, :], in_=ps_cA[:]),
                     waits=[t_cum, hd_ev[b]])
                p.op("vector", vop("tensor_copy", out=cum[:, b, 1, :], in_=ps_cB[:]))
                cum_ev = p.token("vector", "bcume")
                p.op("vector", vop("tensor_tensor",
                                   out=numb.rearrange("p b (h e) -> p b h e", e=64)[:, b],
                                   in0=cum[:, b, 0, 0:1024].rearrange("p (h e) -> p h e", e=64),
                                   in1=CQS[:, i, 0:16][:, :, None].broadcast_to([128, 16, 64]),
                                   op=ALU.mult))
                p.op("vector", vop("tensor_tensor",
                                   out=tmpb.rearrange("p b (h e) -> p b h e", e=64)[:, b],
                                   in0=cum[:, b, 1, 0:1024].rearrange("p (h e) -> p h e", e=64),
                                   in1=CQS[:, i, 16:32][:, :, None].broadcast_to([128, 16, 64]),
                                   op=ALU.mult))
                p.op("vector", vop("tensor_tensor", out=numb[:, b, :], in0=numb[:, b, :],
                                   in1=tmpb[:, b, :], op=ALU.add))
                p.op("vector", vop("tensor_tensor", out=den[:, b, 0, :],
                                   in0=cum[:, b, 0, 1024:1040], in1=CQS[:, i, 0:16],
                                   op=ALU.mult))
                p.op("vector", vop("tensor_tensor", out=den[:, b, 1, :],
                                   in0=cum[:, b, 1, 1024:1040], in1=CQS[:, i, 16:32],
                                   op=ALU.mult))
                p.op("vector", vop("tensor_tensor", out=den[:, b, 0, :],
                                   in0=den[:, b, 0, :], in1=den[:, b, 1, :], op=ALU.add))
                p.op("vector", vop("tensor_scalar", out=den[:, b, 0, :],
                                   in0=den[:, b, 0, :], scalar1=scal[:, 2:3], scalar2=None,
                                   op0=ALU.add))
                p.op("vector", vop("reciprocal", den[:, b, 1, :], den[:, b, 0, :]))
                t_dr = p.token("vector", "bden")
                p.op("vector", vop("tensor_tensor",
                                   out=hds.rearrange("p b (h e) -> p b h e", e=64)[:, b],
                                   in0=numb.rearrange("p b (h e) -> p b h e", e=64)[:, b],
                                   in1=den[:, b, 1, :][:, :, None].broadcast_to([128, 16, 64]),
                                   op=ALU.mult), waits=[t_dr])
                t_hd = p.token("vector", "bhds")
                t_hst = p.op("sync", dma(ao_dram.ap()[i * 128:(i + 1) * 128, :],
                                         hds[:, b, :]),
                             waits=[t_hd], incs=[f"bhst{b}"], dma=True)
                hd_ev[b] = t_hst
                hd_toks.append(t_hst)
        stack = contextlib.ExitStack()
        with stack:
            out_proj(stack, wor, None, hd_toks, add_bias=False)
        p.barrier("b3")

    if n_sub == 94:
        sublayer_moe(0)
    elif n_sub == 95:
        sublayer_moe(0)
        sublayer_moe(1)
    elif n_sub == 13:
        sublayer_attn_a()
        sublayer_attn_b()
    else:
        if n_sub >= 1:
            sublayer_attn_a()
        if n_sub >= 2:
            sublayer_moe(0)
        if n_sub >= 3:
            sublayer_attn_b()
        if n_sub >= 4:
            sublayer_moe(1)

    out_toks = []
    for i in range(NT):
        out_toks.append(p.op("gpsimd", dma(x_out[i * 128:(i + 1) * 128, :], xsb[:, i, :]),
                             incs=["xst"], dma=True))
    p.op("gpsimd", vop("memset", scal[:, 7:8], 0.0), waits=out_toks)

    p.emit()
    return nc


def host_inputs(inputs):
    x = np.ascontiguousarray(inputs["L0_attn_a_x"], np.float32)
    wqkv_ = np.ascontiguousarray(inputs["L0_attn_a_W_qkv"], np.float32)
    bqkv_ = np.ascontiguousarray(inputs["L0_attn_a_b_qkv"], np.float32).reshape(1, -1)
    wproj_ = np.ascontiguousarray(inputs["L0_attn_a_W_proj"], np.float32)
    bproj_ = np.ascontiguousarray(inputs["L0_attn_a_b_proj"], np.float32).reshape(1, -1)
    wq = np.transpose(np.asarray(inputs["L0_attn_b_W_Q"], np.float32), (1, 0, 2)).reshape(D, 16)
    wk = np.transpose(np.asarray(inputs["L0_attn_b_W_K"], np.float32), (1, 0, 2)).reshape(D, 16)
    wqk_ = np.ascontiguousarray(np.concatenate([wq, wk], axis=1))
    wvr_ = np.ascontiguousarray(np.transpose(np.asarray(inputs["L0_attn_b_W_V"], np.float32), (1, 0, 2)).reshape(D, D))
    wor_ = np.ascontiguousarray(np.transpose(np.asarray(inputs["L0_attn_b_W_O"], np.float32), (2, 1, 0)).reshape(D, D))
    wg_ = np.ascontiguousarray(inputs["L0_ffn_Wg"], np.float32)
    w1_ = np.ascontiguousarray(inputs["L0_ffn_w1"], np.float32)
    w2_ = np.ascontiguousarray(inputs["L0_ffn_w2"], np.float32)
    in_maps = []
    for c in range(8):
        in_maps.append({
            "x": x[c],
            "wqkv": wqkv_, "bqkv": bqkv_, "wproj": wproj_, "bproj": bproj_,
            "wqk": wqk_, "wvr": wvr_, "wor": wor_, "wg": wg_,
            "w1s": np.ascontiguousarray(w1_[2 * c:2 * c + 2]),
            "w2s": np.ascontiguousarray(w2_[2 * c:2 * c + 2]),
            "coremask": (np.arange(8) < c).astype(np.float32).reshape(8, 1),
            "corec": np.full((128, 1), c * T, np.float32),
        })
    return in_maps


TWO_LAUNCH = False


def kernel(**inputs):
    in_maps = host_inputs(inputs)
    if not TWO_LAUNCH:
        nc = build(4)
        res = run_bass_kernel_spmd(nc, in_maps, core_ids=list(range(8)))
        return np.stack([np.asarray(r["xout"], np.float32) for r in res.results])
    nc = build(3)
    res = run_bass_kernel_spmd(nc, in_maps, core_ids=list(range(8)))
    mid = [np.asarray(r["xout"], np.float32) for r in res.results]
    nc2 = build(94)
    for c in range(8):
        in_maps[c]["x"] = mid[c]
    res2 = run_bass_kernel_spmd(nc2, in_maps, core_ids=list(range(8)))
    return np.stack([np.asarray(r["xout"], np.float32) for r in res2.results])
